# version 19
# baseline (speedup 1.0000x reference)
import numpy as np
import ml_dtypes
from contextlib import ExitStack

import concourse.bass as bass
import concourse.mybir as mybir
from concourse.bass_utils import run_bass_kernel_spmd

F32 = mybir.dt.float32
BF16 = mybir.dt.bfloat16
AF = mybir.ActivationFunctionType
ALU = mybir.AluOpType
AX = mybir.AxisListType

B, S, D, HID, E = 4, 4096, 1024, 4096, 8
NT = S // 128
NQ = NT // 2
DC = D // 128
CSTAB = 20.0
EPS = 1e-5
INV_SQRT_D = 1.0 / 32.0

DBG = False
DBG_DUMP = set()
PHASES = ("build", "a", "b", "c")


def qtiles_for(par):
    return [g for g in range(NT) if (g % 4) in ((0, 3) if par == 0 else (1, 2))]


class Prog:
    def __init__(self, nc, es):
        self.nc = nc
        self.es = es
        self.sems = {}
        self.cnt = {}

    def sem(self, name):
        h = self.es.enter_context(self.nc.semaphore(name))
        self.sems[name] = h
        self.cnt[name] = 0
        return h

    def inc(self, name, k=1):
        self.cnt[name] += k
        return self.cnt[name]

    def val(self, name):
        return self.cnt[name]


class Emitter:
    ENGS = ("sy", "g", "t", "v", "s")

    def __init__(self):
        self.ops = {e: [] for e in self.ENGS}

    def add(self, eng, fn):
        self.ops[eng].append(fn)

    def wait(self, eng, p, sem, tgt):
        if tgt <= 0:
            return
        h = p.sems[sem]
        self.add(eng, lambda e, h=h, tgt=tgt: e.wait_ge(h, tgt))

    def replay(self, block):
        em = self

        @block.sync
        def _(sy):
            for f in em.ops["sy"]:
                f(sy)

        @block.gpsimd
        def _(g):
            for f in em.ops["g"]:
                f(g)

        @block.tensor
        def _(t):
            for f in em.ops["t"]:
                f(t)

        @block.vector
        def _(v):
            for f in em.ops["v"]:
                f(v)

        @block.scalar
        def _(s):
            for f in em.ops["s"]:
                f(s)


def build_nc():
    nc = bass.Bass("TRN2", target_bir_lowering=False, debug=False)

    xb_d = nc.dram_tensor("xb", [S, D], F32, kind="ExternalInput").ap()
    xq_d = nc.dram_tensor("xq", [NQ, 128, D], F32, kind="ExternalInput").ap()
    cos_d = nc.dram_tensor("cos_t", [S, 512], BF16, kind="ExternalInput").ap()
    sin_d = nc.dram_tensor("sin_t", [S, 512], BF16, kind="ExternalInput").ap()
    cosq_d = nc.dram_tensor("cosq_t", [NQ, 128, 512], BF16, kind="ExternalInput").ap()
    sinq_d = nc.dram_tensor("sinq_t", [NQ, 128, 512], BF16, kind="ExternalInput").ap()
    wq_d = nc.dram_tensor("wq_t", [128, DC * D], F32, kind="ExternalInput").ap()
    wk_d = nc.dram_tensor("wk_t", [128, DC * D], F32, kind="ExternalInput").ap()
    wv_d = nc.dram_tensor("wv_t", [128, DC * D], F32, kind="ExternalInput").ap()
    wg_d = nc.dram_tensor("wg_t", [128, DC * E], F32, kind="ExternalInput").ap()
    win_d = nc.dram_tensor("w_in_r", [HID, D], F32, kind="ExternalInput").ap()
    wot_d = nc.dram_tensor("w_out_tr", [HID, D], F32, kind="ExternalInput").ap()
    masks_d = nc.dram_tensor("masks", [NQ, 128, 512], BF16, kind="ExternalInput").ap()
    idb_d = nc.dram_tensor("ident_bf", [128, 128], BF16, kind="ExternalInput").ap()
    idf_d = nc.dram_tensor("ident_f32", [128, 128], F32, kind="ExternalInput").ap()
    out_d = nc.dram_tensor("out", [NQ, 128, D], F32, kind="ExternalOutput").ap()
    scratch_kind = "ExternalOutput" if DBG else "Internal"
    r1s_d = nc.dram_tensor("r1s", [NQ, 128, D], F32, kind=scratch_kind).ap()
    wcmy_t = nc.dram_tensor("wcmy", [DC * 128, D], BF16, kind="Internal")
    wcg_t = nc.dram_tensor("wcg", [E * DC * 128, D], BF16, kind="Internal")
    wcmy_d = wcmy_t.ap()
    wcg_d = wcg_t.ap()
    qts_d = nc.dram_tensor("qts", [NQ, 128, D], BF16, kind="Internal").ap()

    es0 = ExitStack()
    with es0:
        p = Prog(nc, es0)
        for sname in ("lda0", "lda1", "csl0", "csl1", "csl2", "csl3", "qld0", "qld1", "qst",
                      "ldk", "ldw", "ldwc", "ldc0", "ldc1",
                      "gldb0", "gldb1", "gst", "ccs", "str0", "str1", "sto0", "sto1",
                      "t", "v", "s", "gp",
                      *[f"lwc{e}" for e in range(E)]):
            p.sem(sname)

        def sb(name, shape, dt=F32):
            return es0.enter_context(nc.sbuf_tensor(name, shape, dt))

        ident_bf = sb("ident_bf_sb", [128, 128], BF16)
        ident_f = sb("ident_f_sb", [128, 128], F32)
        wg_sb = sb("wg_sb", [128, DC * E], F32)
        cstab_c = sb("cstab_c", [128, 1], F32)
        eps_c = sb("eps_c", [128, 1], F32)
        def mk_cs(tag):
            return {k: sb(f"cs_{tag}_{k}", [128, 1], F32)
                    for k in ("s", "q", "m", "bt", "veps", "rstd", "nmr")}
        cs_x = [mk_cs("x0"), mk_cs("x1")]
        cs_e = mk_cs("e")
        cs_e3 = mk_cs("e3")
        rstd_c = cs_e["rstd"]
        nmr_c = cs_e["nmr"]
        x_t = [sb(f"x_t{i}", [128, D]) for i in range(2)]
        cos_t = [sb(f"cos_t{i}", [128, 512], BF16) for i in range(4)]
        sin_t = [sb(f"sin_t{i}", [128, 512], BF16) for i in range(4)]
        sq_junk = sb("sq_junk", [128, D], BF16)

        block = es0.enter_context(nc.Block())

        em = Emitter()
        em.add("sy", lambda e: e.dma_start(out=ident_bf[:], in_=idb_d[:]).then_inc(p.sems["ldk"], 16))
        em.add("sy", lambda e: e.dma_start(out=ident_f[:], in_=idf_d[:]).then_inc(p.sems["ldk"], 16))
        em.add("sy", lambda e: e.dma_start(out=wg_sb[:], in_=wg_d[:]).then_inc(p.sems["ldk"], 16))
        CONST_LD = p.inc("ldk", 48)
        em.add("v", lambda e: e.memset(cstab_c[:], -CSTAB).then_inc(p.sems["v"], 1))
        p.inc("v")
        em.add("v", lambda e: e.memset(eps_c[:], EPS).then_inc(p.sems["v"], 1))
        p.inc("v")
        em.replay(block)

        def emit_ln_stats(em, x_ap, ready_sem, ready_tgt, cs, sq_dst):
            em.wait("v", p, ready_sem, ready_tgt)
            def _red(e, cs=cs):
                e.reduce_sum(cs["s"][:], x_ap, axis=AX.X)
                e.drain()
                e.tensor_scalar(cs["bt"][:], cs["s"][:], cs["s"][:], -1.0 / (D * D), ALU.mult, ALU.mult).then_inc(p.sems["v"], 1)
            em.add("v", _red)
            v_bt = p.inc("v")
            em.wait("s", p, ready_sem, ready_tgt)
            em.add("s", lambda e, cs=cs, sq_dst=sq_dst: e.activation(sq_dst, x_ap, AF.Square, accum_out=cs["q"][:]).then_inc(p.sems["s"], 1))
            s_sq = p.inc("s")
            em.wait("s", p, "v", v_bt)
            def _rstd(e, cs=cs):
                e.activation(cs["veps"][:], cs["q"][:], AF.Identity, bias=cs["bt"][:], scale=1.0 / D)
                e.drain()
                e.activation(cs["rstd"][:], cs["veps"][:], AF.Ln, bias=eps_c[:])
                e.drain()
                e.activation(cs["rstd"][:], cs["rstd"][:], AF.Exp, scale=-0.5).then_inc(p.sems["s"], 1)
            em.add("s", _rstd)
            s_rstd = p.inc("s")
            em.wait("g", p, "s", s_rstd)
            def _nmr(e, cs=cs):
                e.tensor_scalar(cs["nmr"][:], cs["s"][:], cs["rstd"][:], -1.0 / D, ALU.mult, ALU.mult).then_inc(p.sems["gp"], 1)
            em.add("g", _nmr)
            g_nmr = p.inc("gp")
            return ("gp", g_nmr), s_sq

        if "build" in PHASES:
            with ExitStack() as esB:
                def sbB(name, shape, dt=F32):
                    return esB.enter_context(nc.sbuf_tensor(name, shape, dt))
                win_hb = [sbB(f"win_hb{i}", [128, 8 * D], BF16) for i in range(2)]
                wot_hb = [sbB(f"wot_hb{i}", [128, 8 * D], BF16) for i in range(2)]
                w_acc = sbB("w_acc", [128, DC * D], F32)
                bld_ps = [esB.enter_context(nc.psum_tensor(f"bld_ps{i}", [128, 512 * 4], F32)) for i in range(2)]

                em = Emitter()
                pre_ld = {}
                pre_stats = {}
                for pt in range(2):
                    def _ldx0(eng, pt=pt):
                        eng.dma_start(out=x_t[pt][:], in_=xb_d[pt * 128:(pt + 1) * 128, :]).then_inc(p.sems[f"lda{pt}"], 16)
                        eng.dma_start(out=cos_t[pt][:], in_=cos_d[pt * 128:(pt + 1) * 128, :]).then_inc(p.sems[f"csl{pt}"], 16)
                        eng.dma_start(out=sin_t[pt][:], in_=sin_d[pt * 128:(pt + 1) * 128, :]).then_inc(p.sems[f"csl{pt}"], 16)
                    em.add("sy", _ldx0)
                    pre_ld[pt] = (p.inc(f"lda{pt}", 16), p.inc(f"csl{pt}", 32))
                grp_free_v = [0, 0]
                buf_free_t = [0, 0]
                for hb in range(4):
                    cur = hb % 2
                    gsem = f"gldb{cur}"
                    em.wait("g", p, "t", buf_free_t[cur])
                    def _ldb(eng, cur=cur, hb=hb, gsem=gsem):
                        eng.dma_start(
                            out=win_hb[cur][:].rearrange("p (hc d) -> p hc d", hc=8),
                            in_=win_d[hb * 1024:(hb + 1) * 1024, :].rearrange("(hc p) d -> p hc d", p=128),
                        ).then_inc(p.sems[gsem], 16)
                        eng.dma_start(
                            out=wot_hb[cur][:].rearrange("p (hc d) -> p hc d", hc=8),
                            in_=wot_d[hb * 1024:(hb + 1) * 1024, :].rearrange("(hc p) d -> p hc d", p=128),
                        ).then_inc(p.sems[gsem], 16)
                    em.add("g", _ldb)
                    g_ld = p.inc(gsem, 32)
                    em.wait("t", p, gsem, g_ld)
                    if hb == 2:
                        pre_stats[0] = emit_ln_stats(em, x_t[0][:], "lda0", pre_ld[0][0], cs_x[0], sq_junk[:])[0]
                        pre_stats[1] = emit_ln_stats(em, x_t[1][:], "lda1", pre_ld[1][0], cs_x[1], sq_junk[:])[0]
                    for oh in range(2):
                        for grp in range(2):
                            em.wait("t", p, "v", grp_free_v[grp])
                            def _mm(eng, cur=cur, oh=oh, grp=grp):
                                last = None
                                for dtl in range(4):
                                    dt_ = grp * 4 + dtl
                                    for hc in range(8):
                                        last = eng.matmul(
                                            bld_ps[grp][:, dtl * 512:(dtl + 1) * 512],
                                            win_hb[cur][:, hc * 1024 + dt_ * 128: hc * 1024 + dt_ * 128 + 128],
                                            wot_hb[cur][:, hc * 1024 + oh * 512: hc * 1024 + oh * 512 + 512],
                                            start=(hc == 0), stop=(hc == 7),
                                        )
                                last.then_inc(p.sems["t"], 1)
                            em.add("t", _mm)
                            t_mm = p.inc("t")
                            em.wait("v", p, "t", t_mm)
                            def _acc(eng, oh=oh, grp=grp, first=(hb == 0)):
                                eng.drain()
                                last = None
                                for dtl in range(4):
                                    dt_ = grp * 4 + dtl
                                    dst = w_acc[:, dt_ * 1024 + oh * 512: dt_ * 1024 + oh * 512 + 512]
                                    src = bld_ps[grp][:, dtl * 512:(dtl + 1) * 512]
                                    if first:
                                        last = eng.tensor_copy(dst, src)
                                    else:
                                        last = eng.tensor_tensor(dst, dst, src, ALU.add)
                                last.then_inc(p.sems["v"], 1)
                            em.add("v", _acc)
                            grp_free_v[grp] = p.inc("v")
                    buf_free_t[cur] = p.val("t")
                em.wait("g", p, "v", p.val("v"))
                def _st(eng):
                    for dc in range(DC):
                        eng.dma_start(out=wcmy_d[dc * 128:(dc + 1) * 128, :], in_=w_acc[:, dc * 1024:(dc + 1) * 1024]).then_inc(p.sems["gst"], 16)
                em.add("g", _st)
                p.inc("gst", 16 * DC)
                em.wait("g", p, "gst", p.val("gst"))
                def _cc(eng):
                    eng.collective_compute(
                        "AllGather",
                        mybir.AluOpType.bypass,
                        replica_groups=[list(range(8))],
                        ins=[wcmy_t.ap().opt()],
                        outs=[wcg_t.ap().opt()],
                    ).then_inc(p.sems["ccs"])
                em.add("g", _cc)
                p.inc("ccs", 1)
                em.replay(block)
        BUILD_T = p.val("t")
        BUILD_V = p.val("v")
        BUILD_GST = p.val("gst")

        r1_store_st = [0] * NQ
        if "a" in PHASES or "b" in PHASES:
            with ExitStack() as esA:
                def sbA(name, shape, dt=F32):
                    return esA.enter_context(nc.sbuf_tensor(name, shape, dt))
                kT_sb = sbA("kT_sb", [128, DC * S], BF16)
                v_sb = sbA("v_sb", [128, NT * D], BF16)
                wkv_sb = sbA("wkv_sb", [128, 2 * DC * D], BF16)
                xn_bf = [sbA(f"xn_bf{i}", [128, D], BF16) for i in range(2)]
                xnT_bf = sbA("xnT_bf", [128, D], BF16)
                rtmp = [sbA(f"rtmp{i}", [128, 512]) for i in range(4)]
                qT_bf = sbA("qT_bf", [128, D], BF16)
                masks_t = [sbA(f"masks_t{i}", [128, 512], BF16) for i in range(2)]
                a_row = [sbA(f"a_row{i}", [128, 512], BF16) for i in range(2)]
                a_tile4 = [sbA(f"a_tile4_{i}", [128, 512], BF16) for i in range(2)]
                rsum = sbA("rsum", [128, 8], F32)
                rinv_c = sbA("rinv_c", [128, 1], F32)
                av_sb = sbA("av_sb", [128, D], F32)
                trp = [esA.enter_context(nc.psum_tensor(f"trp{i}", [128, 1024], BF16)) for i in range(2)]
                kq_ps = esA.enter_context(nc.psum_tensor("kq_ps", [128, 1024], F32))
                va_ps = esA.enter_context(nc.psum_tensor("va_ps", [128, 1024], F32))
                s_ps = [esA.enter_context(nc.psum_tensor(f"s_ps{i}", [128, 512], F32)) for i in range(2)]

                em = Emitter()
                em.wait("s", p, "t", BUILD_T)
                em.wait("s", p, "v", BUILD_V)
                em.wait("s", p, "gst", BUILD_GST)
                em.wait("g", p, "v", BUILD_V)
                em.wait("g", p, "gst", BUILD_GST)
                def _ldw(eng):
                    eng.dma_start(out=wkv_sb[:, 0:DC * D], in_=wk_d[:]).then_inc(p.sems["ldw"], 16)
                    eng.dma_start(out=wkv_sb[:, DC * D:], in_=wv_d[:]).then_inc(p.sems["ldw"], 16)
                em.add("g", _ldw)
                g_w = p.inc("ldw", 32)

                trp_free = [("v", 0), ("v", 0)]
                kq_free_v = [0]
                va_free_s = [0]
                av_gp_prev = [0]
                rope_war = [0]
                xn_free_t = [0, 0]
                x_free = [(None, 0), (None, 0)]
                cs_free = [(None, 0)] * 4
                mask_free = [(None, 0), (None, 0)]

                def emit_trans_evac(src_bf, dst_writer, gate, eng_ev="v"):
                    for gidx in range(2):
                        if gate:
                            em.wait("t", p, gate[0], gate[1])
                        fs, fv = trp_free[gidx]
                        if fv:
                            em.wait("t", p, fs, fv)
                        def _tr(eng, gidx=gidx, src_bf=src_bf):
                            last = None
                            for k in range(4):
                                dc = gidx * 4 + k
                                last = eng.transpose(trp[gidx][:, k * 128:(k + 1) * 128], src_bf[:, dc * 128:(dc + 1) * 128], ident_bf[:])
                            last.then_inc(p.sems["t"], 1)
                        em.add("t", _tr)
                        t_tr = p.inc("t")
                        em.wait(eng_ev, p, "t", t_tr)
                        def _ev(eng, gidx=gidx, dst_writer=dst_writer, eng_ev=eng_ev):
                            dst_writer(eng, gidx).then_inc(p.sems[eng_ev], 1)
                        em.add(eng_ev, _ev)
                        trp_free[gidx] = (eng_ev, p.inc(eng_ev))
                    return (eng_ev, p.val(eng_ev))

                def emit_trans_wait_s(s_gate):
                    em.wait("t", p, "s", s_gate)

                def emit_proj(w_off, ps, free_gate):
                    sem, tgt = free_gate
                    if tgt:
                        em.wait("t", p, sem, tgt)
                    def _proj(eng, w_off=w_off, ps=ps):
                        last = None
                        for oh in range(2):
                            for dc in range(DC):
                                last = eng.matmul(
                                    ps[:, oh * 512:(oh + 1) * 512],
                                    xnT_bf[:, dc * 128:(dc + 1) * 128],
                                    wkv_sb[:, w_off + dc * 1024 + oh * 512: w_off + dc * 1024 + oh * 512 + 512],
                                    start=(dc == 0), stop=(dc == 7),
                                )
                        last.then_inc(p.sems["t"], 1)
                    em.add("t", _proj)
                    return p.inc("t")

                def emit_rope(cur, t_proj, cs_gate=None, ci=None):
                    em.wait("v", p, "t", t_proj)
                    if cs_gate:
                        em.wait("v", p, cs_gate[0], cs_gate[1])
                    if rope_war[0]:
                        em.wait("v", p, "gp", rope_war[0])
                    def _ropev(eng, cur=cur, ci=ci):
                        cos_ap, sin_ap = cos_t[ci][:], sin_t[ci][:]
                        eng.tensor_tensor(rtmp[0][:], kq_ps[:, 0:512], cos_ap, ALU.mult)
                        eng.tensor_tensor(rtmp[1][:], kq_ps[:, 512:1024], sin_ap, ALU.mult)
                        eng.tensor_tensor(rtmp[2][:], kq_ps[:, 0:512], sin_ap, ALU.mult)
                        eng.tensor_tensor(rtmp[3][:], kq_ps[:, 512:1024], cos_ap, ALU.mult).then_inc(p.sems["v"], 1)
                    em.add("v", _ropev)
                    v_mults = p.inc("v")
                    kq_free_v[0] = v_mults
                    cs_free[ci] = ("v", v_mults)
                    em.wait("g", p, "v", v_mults)
                    def _ropeg(eng, cur=cur):
                        xnb = xn_bf[cur]
                        eng.tensor_tensor(xnb[:, 0:512], rtmp[0][:], rtmp[1][:], ALU.subtract)
                        eng.tensor_tensor(xnb[:, 512:1024], rtmp[2][:], rtmp[3][:], ALU.add).then_inc(p.sems["gp"], 1)
                    em.add("g", _ropeg)
                    g_rope = p.inc("gp")
                    rope_war[0] = g_rope
                    return ("gp", g_rope)

                def emit_apply(cur, cs, gate, x_ap=None):
                    em.wait("s", p, gate[0], gate[1])
                    if xn_free_t[cur]:
                        em.wait("s", p, "t", xn_free_t[cur])
                    def _apply(e, cur=cur, cs=cs, x_ap=x_ap):
                        src_ap = x_ap if x_ap is not None else x_t[cur][:]
                        e.activation(xn_bf[cur][:], src_ap, AF.Identity, bias=cs["nmr"][:], scale=cs["rstd"][:]).then_inc(p.sems["s"], 1)
                    em.add("s", _apply)
                    return p.inc("s")

                if "a" in PHASES:
                    stats_v = {}
                    ld_tgts = {}
                    tv_of = {}
                    rope_of = {}

                    def emit_dma_a(pt):
                        cur = pt % 2
                        ci = pt % 4
                        xsem, csem = f"lda{cur}", f"csl{ci}"
                        gate = x_free[cur]
                        if gate[0] is not None:
                            em.wait("sy", p, gate[0], gate[1])
                        def _ldx(eng, pt=pt, cur=cur, xsem=xsem):
                            eng.dma_start(out=x_t[cur][:], in_=xb_d[pt * 128:(pt + 1) * 128, :]).then_inc(p.sems[xsem], 16)
                        em.add("sy", _ldx)
                        x_tgt = p.inc(xsem, 16)
                        gate = cs_free[ci]
                        if gate[0] is not None:
                            em.wait("g", p, gate[0], gate[1])
                        def _ldcs(eng, pt=pt, ci=ci, csem=csem):
                            eng.dma_start(out=cos_t[ci][:], in_=cos_d[pt * 128:(pt + 1) * 128, :]).then_inc(p.sems[csem], 16)
                            eng.dma_start(out=sin_t[ci][:], in_=sin_d[pt * 128:(pt + 1) * 128, :]).then_inc(p.sems[csem], 16)
                        em.add("g", _ldcs)
                        cs_tgt = p.inc(csem, 32)
                        return x_tgt, cs_tgt

                    def _wr_kT_of(ppt):
                        def _wr(eng, gidx, ppt=ppt):
                            last = None
                            for k in range(4):
                                dc = gidx * 4 + k
                                last = eng.copy(kT_sb[:, dc * S + ppt * 128: dc * S + ppt * 128 + 128], trp[gidx][:, k * 128:(k + 1) * 128])
                            return last
                        return _wr

                    def _wr_xnT(eng, gidx):
                        last = None
                        for k in range(4):
                            dc = gidx * 4 + k
                            last = eng.tensor_copy(xnT_bf[:, dc * 128:(dc + 1) * 128], trp[gidx][:, k * 128:(k + 1) * 128])
                        return last

                    ld_tgts[0] = pre_ld[0]
                    ld_tgts[1] = pre_ld[1]
                    stats_v[0] = pre_stats[0]
                    stats_v[1] = pre_stats[1]
                    em.wait("t", p, "ldw", g_w)
                    em.wait("t", p, "ldk", CONST_LD)
                    for pt in range(NT):
                        cur = pt % 2
                        cs = cs_x[cur]
                        s_xn = emit_apply(cur, cs, stats_v[pt])
                        x_free[cur] = ("s", s_xn)
                        if 2 <= pt + 1 < NT:
                            ncur = (pt + 1) % 2
                            stats_v[pt + 1] = emit_ln_stats(em, x_t[ncur][:], f"lda{ncur}", ld_tgts[pt + 1][0], cs_x[ncur], sq_junk[:])[0]
                        if pt >= 1:
                            em.wait("s", p, "t", tv_of[pt - 1])
                            def _vev(eng, ppt=pt - 1):
                                eng.copy(v_sb[:, ppt * D:(ppt + 1) * D], va_ps[:]).then_inc(p.sems["s"], 1)
                            em.add("s", _vev)
                            va_free_s[0] = p.inc("s")
                        emit_trans_wait_s(s_xn)
                        g_xnT = emit_trans_evac(xn_bf[cur], _wr_xnT, None)
                        if pt >= 1:
                            emit_trans_evac(xn_bf[1 - cur], _wr_kT_of(pt - 1), rope_of[pt - 1], eng_ev="s")
                            xn_free_t[1 - cur] = p.val("t")
                        em.wait("t", p, g_xnT[0], g_xnT[1])
                        t_kproj = emit_proj(0, kq_ps, ("v", kq_free_v[0]))
                        tv_of[pt] = emit_proj(DC * D, va_ps, ("s", va_free_s[0]))
                        rope_of[pt] = emit_rope(cur, t_kproj, (f"csl{pt % 4}", ld_tgts[pt][1]), ci=pt % 4)
                        if pt + 2 < NT:
                            ld_tgts[pt + 2] = emit_dma_a(pt + 2)
                    emit_trans_evac(xn_bf[(NT - 1) % 2], _wr_kT_of(NT - 1), rope_of[NT - 1], eng_ev="s")
                    xn_free_t[(NT - 1) % 2] = p.val("t")
                    em.wait("s", p, "t", tv_of[NT - 1])
                    def _vev_last(eng):
                        eng.copy(v_sb[:, (NT - 1) * D:NT * D], va_ps[:]).then_inc(p.sems["s"], 1)
                    em.add("s", _vev_last)
                    va_free_s[0] = p.inc("s")
                PH_A_T = p.val("t")

                if "b" in PHASES:
                    if "a" not in PHASES:
                        em.wait("t", p, "ldw", g_w)
                        em.wait("t", p, "ldk", CONST_LD)
                    em.wait("g", p, "t", PH_A_T)
                    def _ldq(eng):
                        eng.dma_start(out=wkv_sb[:, 0:DC * D], in_=wq_d[:]).then_inc(p.sems["ldw"], 16)
                    em.add("g", _ldq)
                    g_wq = p.inc("ldw", 16)
                    a_row_free_t = [0, 0]
                    a_tile_free_t = [0, 0]
                    s_ps_free_s = [0, 0, 0, 0]
                    sps_def = [(s_ps[0], 0), (s_ps[1], 0), (kq_ps, 0), (kq_ps, 512)]
                    statsb_v = {}
                    ldb_tgts = {}

                    def emit_dma_b(j):
                        cur = j % 2
                        ci = j % 4
                        xsem, csem = f"lda{cur}", f"csl{ci}"
                        gate = x_free[cur]
                        if gate[0] is not None:
                            em.wait("sy", p, gate[0], gate[1])
                        def _ldx(eng, j=j, cur=cur, xsem=xsem):
                            eng.dma_start(out=x_t[cur][:], in_=xq_d[j]).then_inc(p.sems[xsem], 16)
                        em.add("sy", _ldx)
                        x_tgt = p.inc(xsem, 16)
                        for gate in (cs_free[ci], mask_free[cur]):
                            if gate[0] is not None:
                                em.wait("g", p, gate[0], gate[1])
                        def _ldcs(eng, j=j, cur=cur, ci=ci, csem=csem):
                            eng.dma_start(out=cos_t[ci][:], in_=cosq_d[j]).then_inc(p.sems[csem], 16)
                            eng.dma_start(out=sin_t[ci][:], in_=sinq_d[j]).then_inc(p.sems[csem], 16)
                            eng.dma_start(out=masks_t[cur][:], in_=masks_d[j]).then_inc(p.sems[csem], 16)
                        em.add("g", _ldcs)
                        cs_tgt = p.inc(csem, 48)
                        return x_tgt, cs_tgt

                    ldb_tgts[0] = emit_dma_b(0)
                    ldb_tgts[1] = emit_dma_b(1)
                    statsb_v[0] = emit_ln_stats(em, x_t[0][:], "lda0", ldb_tgts[0][0], cs_x[0], sq_junk[:])[0]
                    for j in range(NQ):
                        nblk = j // 2 + 1
                        nkt = 4 * nblk
                        cur = j % 2
                        cs = cs_x[cur]
                        if j == 0:
                            em.wait("t", p, "ldw", g_wq)
                        s_xn = emit_apply(cur, cs, statsb_v[j])
                        if j + 1 < NQ:
                            ncur = (j + 1) % 2
                            statsb_v[j + 1] = emit_ln_stats(em, x_t[ncur][:], f"lda{ncur}", ldb_tgts[j + 1][0], cs_x[ncur], sq_junk[:])[0]
                        emit_trans_wait_s(s_xn)
                        def _wr_xnT(eng, gidx):
                            last = None
                            for k in range(4):
                                dc = gidx * 4 + k
                                last = eng.tensor_copy(xnT_bf[:, dc * 128:(dc + 1) * 128], trp[gidx][:, k * 128:(k + 1) * 128])
                            return last
                        g_xnT = emit_trans_evac(xn_bf[cur], _wr_xnT, None)
                        em.wait("t", p, g_xnT[0], g_xnT[1])
                        em.wait("t", p, "s", s_ps_free_s[2])
                        em.wait("t", p, "s", s_ps_free_s[3])
                        t_qproj = emit_proj(0, kq_ps, ("v", kq_free_v[0]))
                        g_rope = emit_rope(cur, t_qproj, (f"csl{j % 4}", ldb_tgts[j][1]), ci=j % 4)
                        def _wr_qT(eng, gidx):
                            last = None
                            for k in range(4):
                                dc = gidx * 4 + k
                                last = eng.tensor_copy(qT_bf[:, dc * 128:(dc + 1) * 128], trp[gidx][:, k * 128:(k + 1) * 128])
                            return last
                        g_qT = emit_trans_evac(xn_bf[cur], _wr_qT, g_rope)
                        v_qT = g_qT[1]
                        xn_free_t[cur] = p.val("t")
                        s_exp_last = 0
                        exp_of = {}

                        eva_of = {}

                        def emit_tra(blk):
                            tc = blk % 2
                            em.wait("t", p, "s", exp_of[blk])
                            fs, fv = trp_free[tc]
                            if fv:
                                em.wait("t", p, fs, fv)
                            def _tra(eng, blk=blk, tc=tc):
                                last = None
                                for sub in range(4):
                                    last = eng.transpose(trp[tc][:, sub * 128:(sub + 1) * 128], a_row[blk % 2][:, sub * 128:(sub + 1) * 128], ident_bf[:])
                                last.then_inc(p.sems["t"], 1)
                            em.add("t", _tra)
                            t_tra = p.inc("t")
                            a_row_free_t[blk % 2] = t_tra
                            em.wait("v", p, "t", t_tra)
                            em.wait("v", p, "t", a_tile_free_t[tc])
                            def _eva(eng, tc=tc):
                                eng.tensor_copy(a_tile4[tc][:], trp[tc][:, 0:512]).then_inc(p.sems["v"], 1)
                            em.add("v", _eva)
                            eva_of[blk] = p.inc("v")
                            trp_free[tc] = ("v", eva_of[blk])

                        def emit_av(blk, nkt=nkt):
                            tc = blk % 2
                            em.wait("t", p, "v", eva_of[blk])
                            if blk == 0:
                                em.wait("t", p, "s", va_free_s[0])
                            def _av(eng, blk=blk, tc=tc, nkt=nkt):
                                last = None
                                for sub in range(4):
                                    kt_g = blk * 4 + sub
                                    first = (kt_g == 0)
                                    lastf = (kt_g == nkt - 1)
                                    eng.matmul(va_ps[:, 0:512], a_tile4[tc][:, sub * 128:(sub + 1) * 128], v_sb[:, kt_g * D: kt_g * D + 512], start=first, stop=lastf)
                                    last = eng.matmul(va_ps[:, 512:1024], a_tile4[tc][:, sub * 128:(sub + 1) * 128], v_sb[:, kt_g * D + 512: kt_g * D + 1024], start=first, stop=lastf)
                                last.then_inc(p.sems["t"], 1)
                            em.add("t", _av)
                            a_tile_free_t[tc] = p.inc("t")

                        for blk in range(nblk):
                            bc = blk % 4
                            ar = blk % 2
                            spt, spo = sps_def[bc]
                            em.wait("t", p, "v", v_qT)
                            em.wait("t", p, "s", s_ps_free_s[bc])
                            last_blk = (blk == nblk - 1)
                            def _sc(eng, blk=blk, cur=cur, last_blk=last_blk, spt=spt, spo=spo):
                                last = None
                                for dc in range(DC):
                                    last = eng.matmul(
                                        spt[:, spo:spo + 512],
                                        qT_bf[:, dc * 128:(dc + 1) * 128],
                                        kT_sb[:, dc * S + blk * 512: dc * S + blk * 512 + 512],
                                        start=(dc == 0), stop=(dc == 7 and not last_blk),
                                    )
                                if last_blk:
                                    for sub in range(4):
                                        last = eng.matmul(
                                            spt[:, spo + sub * 128:spo + (sub + 1) * 128],
                                            masks_t[cur][:, sub * 128:(sub + 1) * 128],
                                            ident_bf[:],
                                            start=False, stop=(sub == 3),
                                        )
                                last.then_inc(p.sems["t"], 1)
                            em.add("t", _sc)
                            t_sc = p.inc("t")
                            if last_blk:
                                mask_free[cur] = ("t", t_sc)
                            em.wait("s", p, "t", t_sc)
                            em.wait("s", p, "t", a_row_free_t[ar])
                            def _exp(eng, blk=blk, ar=ar, spt=spt, spo=spo):
                                eng.activation(a_row[ar][:], spt[:, spo:spo + 512], AF.Exp,
                                               bias=cstab_c[:], scale=INV_SQRT_D,
                                               accum_out=rsum[:, blk:blk + 1]).then_inc(p.sems["s"], 1)
                            em.add("s", _exp)
                            s_exp = p.inc("s")
                            exp_of[blk] = s_exp
                            s_ps_free_s[bc] = s_exp
                            s_exp_last = s_exp
                            if blk >= 1:
                                emit_tra(blk - 1)
                            if blk >= 2:
                                emit_av(blk - 2)
                        emit_tra(nblk - 1)
                        if nblk >= 2:
                            emit_av(nblk - 2)
                        emit_av(nblk - 1)
                        t_av_done = p.val("t")
                        em.wait("v", p, "s", s_exp_last)
                        def _rs(eng, nblk=nblk):
                            eng.reduce_sum(rinv_c[:], rsum[:, 0:nblk], axis=AX.X)
                            eng.drain()
                            eng.reciprocal(rinv_c[:], rinv_c[:]).then_inc(p.sems["v"], 1)
                        em.add("v", _rs)
                        v_rinv = p.inc("v")
                        em.wait("s", p, "v", v_rinv)
                        em.wait("s", p, "t", t_av_done)
                        if av_gp_prev[0]:
                            em.wait("s", p, "gp", av_gp_prev[0])
                        def _avev(eng):
                            eng.drain()
                            eng.activation(av_sb[:], va_ps[:], AF.Identity, scale=rinv_c[:]).then_inc(p.sems["s"], 1)
                        em.add("s", _avev)
                        s_avev = p.inc("s")
                        va_free_s[0] = s_avev
                        if DBG and "slot0" in DBG_DUMP and j == 0:
                            d0 = nc.dram_tensor("dbg_s0", [128, 1024 + 512 + 1024 + 16], F32, kind="ExternalOutput").ap()
                            d0b = nc.dram_tensor("dbg_s0b", [128, 1536], BF16, kind="ExternalOutput").ap()
                            em.wait("sy", p, "s", s_avev)
                            em.wait("sy", p, "v", p.val("v"))
                            def _dmp(eng):
                                eng.dma_start(out=d0b[:, 0:1024], in_=qT_bf[:]).then_inc(p.sems["sto0"], 16)
                                eng.dma_start(out=d0b[:, 1024:1536], in_=a_row[0][:]).then_inc(p.sems["sto0"], 16)
                                eng.dma_start(out=d0[:, 0:1024], in_=av_sb[:]).then_inc(p.sems["sto0"], 16)
                                eng.dma_start(out=d0[:, 1024:1032], in_=rsum[:], allow_slow_non_contiguous=True).then_inc(p.sems["sto0"], 16)
                                eng.dma_start(out=d0[:, 1032:1033], in_=rinv_c[:], allow_slow_non_contiguous=True).then_inc(p.sems["sto0"], 16)
                            em.add("sy", _dmp)
                            dump_tgt = p.inc("sto0", 80)
                            for engx in ("v", "s", "t"):
                                em.wait(engx, p, "sto0", dump_tgt)
                        g_nmr_av, _ = emit_ln_stats(em, av_sb[:], "s", s_avev, cs_e, sq_junk[:])
                        av_gp_prev[0] = g_nmr_av[1]
                        em.wait("v", p, g_nmr_av[0], g_nmr_av[1])
                        em.wait("v", p, "gp", rope_war[0])
                        def _r1(eng, cur=cur):
                            eng.tensor_scalar(rtmp[0][:], x_t[cur][:, 0:512], cs_e["nmr"][:], None, ALU.add)
                            eng.tensor_scalar(rtmp[1][:], x_t[cur][:, 512:1024], cs_e["nmr"][:], None, ALU.add)
                            eng.drain()
                            eng.scalar_tensor_tensor(x_t[cur][:, 0:512], av_sb[:, 0:512], cs_e["rstd"][:], rtmp[0][:], ALU.mult, ALU.add)
                            eng.scalar_tensor_tensor(x_t[cur][:, 512:1024], av_sb[:, 512:1024], cs_e["rstd"][:], rtmp[1][:], ALU.mult, ALU.add).then_inc(p.sems["v"], 1)
                        em.add("v", _r1)
                        v_r1 = p.inc("v")
                        em.wait("s", p, "v", v_r1)
                        def _str1(eng, j=j, cur=cur):
                            eng.dma_start(out=r1s_d[j], in_=x_t[cur][:]).then_inc(p.sems["str0"], 16)
                        em.add("s", _str1)
                        r1_store_st[j] = p.inc("str0", 16)
                        x_free[cur] = ("str0", r1_store_st[j])
                        if j + 2 < NQ:
                            ldb_tgts[j + 2] = emit_dma_b(j + 2)
                if DBG and "kv" in DBG_DUMP:
                    kv_d = nc.dram_tensor("dbg_kv", [2, 128, DC * S], BF16, kind="ExternalOutput").ap()
                    em.wait("sy", p, "t", p.val("t"))
                    em.wait("sy", p, "v", p.val("v"))
                    em.wait("sy", p, "s", p.val("s"))
                    def _dump(eng):
                        eng.dma_start(out=kv_d[0], in_=kT_sb[:]).then_inc(p.sems["sto0"], 16)
                        eng.dma_start(out=kv_d[1], in_=v_sb[:]).then_inc(p.sems["sto0"], 16)
                    em.add("sy", _dump)
                    p.inc("sto0", 32)
                em.replay(block)
        PH_B_T, PH_B_V, PH_B_S = p.val("t"), p.val("v"), p.val("s")

        if "c" in PHASES:
            with ExitStack() as esC:
                def sbC(name, shape, dt=F32):
                    return esC.enter_context(nc.sbuf_tensor(name, shape, dt))
                wc_sb = sbC("wc_sb", [128, E * DC * D], BF16)
                xhT_bf = sbC("xhT_bf", [128, DC * 2 * 128], BF16)
                gates = sbC("gates", [128, NQ * E], F32)
                r1c_t = [sbC(f"r1c_t{i}", [128, D]) for i in range(2)]
                xh2_t = sbC("xh2_t", [128, D], F32)
                x2T_f = sbC("x2T_f", [128, D], F32)
                lg_sb = sbC("lg_sb", [128, E], F32)
                g_m1 = sbC("g_m1", [128, 1], F32)
                g_nm1 = sbC("g_nm1", [128, 1], F32)
                g_eq = sbC("g_eq", [128, E], F32)
                g_lg2 = sbC("g_lg2", [128, E], F32)
                g_m2 = sbC("g_m2", [128, 1], F32)
                g_ee = sbC("g_ee", [128, E], F32)
                g_dex = sbC("g_dex", [128, 1], F32)
                g_sel = sbC("g_sel", [128, E], F32)
                p_acc = [sbC(f"p_acc{i}", [128, D]) for i in range(2)]
                out_t = [sbC(f"out_t{i}", [128, D]) for i in range(2)]
                xmb2_t = sbC("xmb2_t", [128, D], F32)
                trf = [esC.enter_context(nc.psum_tensor(f"trf{i}", [128, 512], F32)) for i in range(2)]
                lg_ps = esC.enter_context(nc.psum_tensor("lg_ps", [128, 512], F32))
                pp_ps = [esC.enter_context(nc.psum_tensor(f"pp_ps{i}", [128, 1024], F32)) for i in range(2)]

                em = Emitter()
                em.wait("sy", p, "t", PH_B_T)
                em.wait("sy", p, "v", PH_B_V)
                em.wait("sy", p, "s", PH_B_S)
                em.wait("sy", p, "gst", BUILD_GST)
                em.wait("sy", p, "str0", p.val("str0"))
                em.wait("sy", p, "str1", p.val("str1"))
                em.wait("g", p, "t", PH_B_T)
                em.wait("g", p, "v", PH_B_V)
                em.wait("g", p, "s", PH_B_S)
                em.wait("g", p, "ccs", 1)
                def _ldwc(eng):
                    for e_i in range(E):
                        eng.dma_start(
                            out=wc_sb[:, e_i * DC * D:(e_i + 1) * DC * D].rearrange("p (dc d) -> p dc d", dc=DC),
                            in_=wcg_d[e_i * DC * 128:(e_i + 1) * DC * 128, :].rearrange("(dc q) d -> q dc d", q=128),
                        ).then_inc(p.sems[f"lwc{e_i}"], 16)
                em.add("g", _ldwc)
                for e_i in range(E):
                    p.inc(f"lwc{e_i}", 16)

                trf_free_v = [0, 0]
                r1c_free_v = [0, 0]
                pp_free_v = [0, 0]
                lg_free_v = [0]
                out_free_st = [0, 0]
                sq_ln3_prev = [0]
                gp_ln3_prev = [0]
                xh2_free_t = [0]
                x2T_free_t = [0]
                x2T_free_s = [0]
                xh_free_t = [0, 0]

                def emit_c_prologue(j):
                    cur = j % 2
                    csem = f"ldc{cur}"
                    em.wait("sy", p, "v", r1c_free_v[cur])
                    def _ldr(eng, j=j, cur=cur):
                        eng.dma_start(out=r1c_t[cur][:], in_=r1s_d[j]).then_inc(p.sems[csem], 16)
                    em.add("sy", _ldr)
                    ld_r1 = p.inc(csem, 16)
                    gate2, _ = emit_ln_stats(em, r1c_t[cur][:], csem, ld_r1, cs_e, sq_junk[:])
                    em.wait("s", p, gate2[0], gate2[1])
                    if xh2_free_t[0]:
                        em.wait("s", p, "t", xh2_free_t[0])
                    def _ap2(eng, cur=cur):
                        eng.activation(xh2_t[:], r1c_t[cur][:], AF.Identity, bias=nmr_c[:], scale=rstd_c[:]).then_inc(p.sems["s"], 1)
                    em.add("s", _ap2)
                    s_xh2 = p.inc("s")
                    for gidx in range(2):
                        em.wait("t", p, "s", s_xh2)
                        em.wait("t", p, "v", trf_free_v[gidx])
                        def _trf(eng, gidx=gidx):
                            last = None
                            for k in range(4):
                                dc = gidx * 4 + k
                                last = eng.transpose(trf[gidx][:, k * 128:(k + 1) * 128], xh2_t[:, dc * 128:(dc + 1) * 128], ident_f[:])
                            last.then_inc(p.sems["t"], 1)
                        em.add("t", _trf)
                        t_trf = p.inc("t")
                        em.wait("v", p, "t", t_trf)
                        if x2T_free_t[0]:
                            em.wait("v", p, "t", x2T_free_t[0])
                        if x2T_free_s[0]:
                            em.wait("v", p, "s", x2T_free_s[0])
                        def _evf(eng, gidx=gidx):
                            last = None
                            for k in range(4):
                                dc = gidx * 4 + k
                                last = eng.tensor_copy(x2T_f[:, dc * 128:(dc + 1) * 128], trf[gidx][:, k * 128:(k + 1) * 128])
                            last.then_inc(p.sems["v"], 1)
                        em.add("v", _evf)
                        trf_free_v[gidx] = p.inc("v")
                    xh2_free_t[0] = p.val("t")
                    v_x2T = p.val("v")
                    em.wait("s", p, "v", v_x2T)
                    if xh_free_t[j % 2]:
                        em.wait("s", p, "t", xh_free_t[j % 2])
                    def _cast(eng, j=j):
                        last = None
                        for dc in range(DC):
                            last = eng.copy(xhT_bf[:, dc * 256 + (j % 2) * 128: dc * 256 + (j % 2) * 128 + 128], x2T_f[:, dc * 128:(dc + 1) * 128])
                        last.then_inc(p.sems["s"], 1)
                    em.add("s", _cast)
                    s_xh = p.inc("s")
                    em.wait("t", p, "v", v_x2T)
                    em.wait("t", p, "v", lg_free_v[0])
                    if j == 0:
                        em.wait("t", p, "ldk", CONST_LD)
                    def _lg(eng):
                        last = None
                        for dc in range(DC):
                            last = eng.matmul(lg_ps[:, 0:E], x2T_f[:, dc * 128:(dc + 1) * 128], wg_sb[:, dc * E:(dc + 1) * E], start=(dc == 0), stop=(dc == 7))
                        last.then_inc(p.sems["t"], 1)
                    em.add("t", _lg)
                    t_lg = p.inc("t")
                    x2T_free_t[0] = t_lg
                    x2T_free_s[0] = s_xh
                    em.wait("v", p, "t", t_lg)
                    def _gate(eng, j=j):
                        eng.tensor_copy(lg_sb[:], lg_ps[:, 0:E])
                        eng.drain()
                        eng.reduce_max(g_m1[:], lg_sb[:], axis=AX.X)
                        eng.drain()
                        eng.tensor_scalar_mul(g_nm1[:], g_m1[:], -1.0)
                        eng.tensor_scalar(g_eq[:], lg_sb[:], g_m1[:], None, ALU.is_equal)
                        eng.drain()
                        eng.scalar_tensor_tensor(g_lg2[:], g_eq[:], -1e30, lg_sb[:], ALU.mult, ALU.add)
                        eng.drain()
                        eng.reduce_max(g_m2[:], g_lg2[:], axis=AX.X).then_inc(p.sems["v"], 1)
                    em.add("v", _gate)
                    v_g1 = p.inc("v")
                    lg_free_v[0] = v_g1
                    em.wait("s", p, "v", v_g1)
                    def _gexp(eng):
                        eng.activation(g_ee[:], lg_sb[:], AF.Exp, bias=g_nm1[:])
                        eng.drain()
                        eng.activation(g_dex[:], g_m2[:], AF.Exp, bias=g_nm1[:]).then_inc(p.sems["s"], 1)
                    em.add("s", _gexp)
                    s_gexp = p.inc("s")
                    em.wait("v", p, "s", s_gexp)
                    def _gfin(eng, j=j):
                        eng.drain()
                        eng.tensor_scalar_add(g_dex[:], g_dex[:], 1.0)
                        eng.drain()
                        eng.reciprocal(g_dex[:], g_dex[:])
                        eng.tensor_scalar(g_sel[:], lg_sb[:], g_m2[:], None, ALU.is_ge)
                        eng.drain()
                        eng.tensor_tensor(g_ee[:], g_ee[:], g_sel[:], ALU.mult)
                        eng.drain()
                        eng.tensor_scalar(gates[:, j * E:(j + 1) * E], g_ee[:], g_dex[:], None, ALU.mult).then_inc(p.sems["v"], 1)
                    em.add("v", _gfin)
                    v_gates = p.inc("v")
                    return s_xh, v_gates

                sxh_j, vgates_j = emit_c_prologue(0)
                for j in range(NQ):
                    cur = j % 2
                    for e_i in range(E):
                        pc = e_i % 2
                        em.wait("t", p, "v", pp_free_v[pc])
                        em.wait("t", p, "s", sxh_j)
                        if j == 0:
                            em.wait("t", p, f"lwc{e_i}", 16)
                        def _pmm(eng, j=j, e_i=e_i, pc=pc):
                            last = None
                            for oh in range(2):
                                for dc in range(DC):
                                    last = eng.matmul(
                                        pp_ps[pc][:, oh * 512:(oh + 1) * 512],
                                        xhT_bf[:, dc * 256 + (j % 2) * 128: dc * 256 + (j % 2) * 128 + 128],
                                        wc_sb[:, (e_i * DC + dc) * D + oh * 512: (e_i * DC + dc) * D + oh * 512 + 512],
                                        start=(dc == 0), stop=(dc == 7),
                                    )
                            last.then_inc(p.sems["t"], 1)
                        em.add("t", _pmm)
                        t_pmm = p.inc("t")
                        em.wait("v", p, "t", t_pmm)
                        em.wait("v", p, "v", vgates_j)
                        if e_i == 1 and sq_ln3_prev[0] > 0:
                            em.wait("v", p, "s", sq_ln3_prev[0])
                            em.wait("v", p, "gp", gp_ln3_prev[0])
                        def _pacc(eng, j=j, e_i=e_i, pc=pc):
                            eng.drain()
                            gcol = gates[:, j * E + e_i: j * E + e_i + 1]
                            if e_i == 0:
                                eng.tensor_scalar(p_acc[0][:], pp_ps[0][:], gcol, None, ALU.mult).then_inc(p.sems["v"], 1)
                            else:
                                eng.scalar_tensor_tensor(p_acc[e_i % 2][:], pp_ps[pc][:], gcol, p_acc[(e_i + 1) % 2][:], ALU.mult, ALU.add).then_inc(p.sems["v"], 1)
                        em.add("v", _pacc)
                        pp_free_v[pc] = p.inc("v")
                    xh_free_t[j % 2] = p.val("t")
                    v_pdone = p.val("v")
                    if j + 1 < NQ:
                        sxh_j, vgates_j = emit_c_prologue(j + 1)
                    gate3, s_sq3 = emit_ln_stats(em, p_acc[1][:], "v", v_pdone, cs_e3, sq_junk[:])
                    sq_ln3_prev[0] = s_sq3
                    gp_ln3_prev[0] = gate3[1]
                    em.wait("v", p, gate3[0], gate3[1])
                    if out_free_st[cur] > 0:
                        em.wait("v", p, f"sto{cur}", out_free_st[cur])
                    def _fin(eng, cur=cur):
                        eng.drain()
                        eng.tensor_scalar(xmb2_t[:], r1c_t[cur][:], cs_e3["nmr"][:], None, ALU.add)
                        eng.drain()
                        eng.scalar_tensor_tensor(out_t[cur][:], p_acc[1][:], cs_e3["rstd"][:], xmb2_t[:], ALU.mult, ALU.add).then_inc(p.sems["v"], 1)
                    em.add("v", _fin)
                    v_fin = p.inc("v")
                    r1c_free_v[cur] = v_fin
                    em.wait("s", p, "v", v_fin)
                    def _sto(eng, j=j, cur=cur):
                        eng.dma_start(out=out_d[j], in_=out_t[cur][:]).then_inc(p.sems[f"sto{cur}"], 16)
                    em.add("s", _sto)
                    out_free_st[cur] = p.inc(f"sto{cur}", 16)
                for sname in ("sto0", "sto1", "str0", "str1", "gst"):
                    em.wait("s", p, sname, p.val(sname))
                em.replay(block)
        else:
            em = Emitter()
            for sname in ("sto0", "sto1", "str0", "str1", "gst"):
                em.wait("s", p, sname, p.val(sname))
            em.replay(block)

    return nc


def _host_prep(L0_attn_x, L0_attn_Wq, L0_attn_Wk, L0_attn_Wv, L0_ffn_w_gate, L0_ffn_w_in, L0_ffn_w_out):
    x = np.asarray(L0_attn_x, dtype=np.float32)
    Wq = np.asarray(L0_attn_Wq, dtype=np.float32)
    Wk = np.asarray(L0_attn_Wk, dtype=np.float32)
    Wv = np.asarray(L0_attn_Wv, dtype=np.float32)
    wg = np.asarray(L0_ffn_w_gate, dtype=np.float32)
    w_in = np.ascontiguousarray(np.asarray(L0_ffn_w_in, dtype=np.float32))
    w_out = np.asarray(L0_ffn_w_out, dtype=np.float32)

    perm = np.concatenate([np.arange(0, D, 2), np.arange(1, D, 2)])

    def chunk_w(wt):
        return np.ascontiguousarray(wt.reshape(DC, 128, wt.shape[1]).transpose(1, 0, 2).reshape(128, DC * wt.shape[1]))

    wq_t = chunk_w(Wq.T[:, perm])
    wk_t = chunk_w(Wk.T[:, perm])
    wv_t = chunk_w(np.ascontiguousarray(Wv.T))
    wg_t = chunk_w(np.ascontiguousarray(wg.T))

    pos = np.arange(S, dtype=np.float64)
    inv = 10000.0 ** (-np.arange(0, D, 2, dtype=np.float64) / D)
    ang = pos[:, None] * inv[None, :]
    cos_t = np.cos(ang).astype(ml_dtypes.bfloat16)
    sin_t = np.sin(ang).astype(ml_dtypes.bfloat16)

    ident_bf = np.eye(128, dtype=ml_dtypes.bfloat16)
    ident_f = np.eye(128, dtype=np.float32)

    kk = np.arange(128)
    causalT = np.where(kk[:, None] <= kk[None, :], 0.0, -1e30).astype(ml_dtypes.bfloat16)
    allT = np.full((128, 128), -1e30, dtype=np.float32).astype(ml_dtypes.bfloat16)

    in_maps = []
    qlists = []
    for c in range(8):
        b, par = c // 2, c % 2
        qt = qtiles_for(par)
        qlists.append(qt)
        xb = np.ascontiguousarray(x[b])
        xq = np.ascontiguousarray(xb.reshape(NT, 128, D)[qt])
        cosq = np.ascontiguousarray(cos_t.reshape(NT, 128, 512)[qt])
        sinq = np.ascontiguousarray(sin_t.reshape(NT, 128, 512)[qt])
        masks = np.zeros((NQ, 128, 512), dtype=ml_dtypes.bfloat16)
        for jj, g in enumerate(qt):
            nblk = jj // 2 + 1
            base = (nblk - 1) * 4
            for sub in range(4):
                kt = base + sub
                if kt == g:
                    masks[jj, :, sub * 128:(sub + 1) * 128] = causalT
                elif kt > g:
                    masks[jj, :, sub * 128:(sub + 1) * 128] = allT
        in_maps.append({
            "xb": xb, "xq": xq,
            "cos_t": cos_t, "sin_t": sin_t,
            "cosq_t": cosq, "sinq_t": sinq,
            "wq_t": wq_t, "wk_t": wk_t, "wv_t": wv_t, "wg_t": wg_t,
            "w_in_r": np.ascontiguousarray(w_in[c]),
            "w_out_tr": np.ascontiguousarray(w_out[c].T),
            "masks": masks,
            "ident_bf": ident_bf, "ident_f32": ident_f,
        })
    return in_maps, qlists


def kernel(L0_attn_x, L0_attn_Wq, L0_attn_Wk, L0_attn_Wv, L0_ffn_w_gate, L0_ffn_w_in, L0_ffn_w_out):
    in_maps, qlists = _host_prep(L0_attn_x, L0_attn_Wq, L0_attn_Wk, L0_attn_Wv,
                                 L0_ffn_w_gate, L0_ffn_w_in, L0_ffn_w_out)
    nc = build_nc()
    res = run_bass_kernel_spmd(nc, in_maps, core_ids=list(range(8)))
    out = np.empty((B, S, D), dtype=np.float32)
    for c in range(8):
        b = c // 2
        o = res.results[c]["out"]
        for jj, g in enumerate(qlists[c]):
            out[b, g * 128:(g + 1) * 128, :] = o[jj]
    return out


# version 21 (final)
# speedup vs baseline: 1.1554x; 1.1554x over previous
import numpy as np
import ml_dtypes
from contextlib import ExitStack

import concourse.bass as bass
import concourse.mybir as mybir
from concourse.bass_utils import run_bass_kernel_spmd

F32 = mybir.dt.float32
BF16 = mybir.dt.bfloat16
AF = mybir.ActivationFunctionType
ALU = mybir.AluOpType
AX = mybir.AxisListType

B, S, D, HID, E = 4, 4096, 1024, 4096, 8
NT = S // 128
NQ = NT // 2
DC = D // 128
CSTAB = 20.0
EPS = 1e-5
INV_SQRT_D = 1.0 / 32.0

DBG = False
DBG_DUMP = set()
PHASES = ("build", "a", "b", "c")


def qtiles_for(par):
    return [g for g in range(NT) if (g % 4) in ((0, 3) if par == 0 else (1, 2))]


class Prog:
    def __init__(self, nc, es):
        self.nc = nc
        self.es = es
        self.sems = {}
        self.cnt = {}

    def sem(self, name):
        h = self.es.enter_context(self.nc.semaphore(name))
        self.sems[name] = h
        self.cnt[name] = 0
        return h

    def inc(self, name, k=1):
        self.cnt[name] += k
        return self.cnt[name]

    def val(self, name):
        return self.cnt[name]


class Emitter:
    ENGS = ("sy", "g", "t", "v", "s")

    def __init__(self):
        self.ops = {e: [] for e in self.ENGS}

    def add(self, eng, fn):
        self.ops[eng].append(fn)

    def wait(self, eng, p, sem, tgt):
        if tgt <= 0:
            return
        h = p.sems[sem]
        self.add(eng, lambda e, h=h, tgt=tgt: e.wait_ge(h, tgt))

    def replay(self, block):
        em = self

        @block.sync
        def _(sy):
            for f in em.ops["sy"]:
                f(sy)

        @block.gpsimd
        def _(g):
            for f in em.ops["g"]:
                f(g)

        @block.tensor
        def _(t):
            for f in em.ops["t"]:
                f(t)

        @block.vector
        def _(v):
            for f in em.ops["v"]:
                f(v)

        @block.scalar
        def _(s):
            for f in em.ops["s"]:
                f(s)


def build_nc():
    nc = bass.Bass("TRN2", target_bir_lowering=False, debug=False)

    xb_d = nc.dram_tensor("xb", [S, D], F32, kind="ExternalInput").ap()
    xq_d = nc.dram_tensor("xq", [NQ, 128, D], F32, kind="ExternalInput").ap()
    cos_d = nc.dram_tensor("cos_t", [S, 512], F32, kind="ExternalInput").ap()
    sin_d = nc.dram_tensor("sin_t", [S, 512], F32, kind="ExternalInput").ap()
    cosq_d = nc.dram_tensor("cosq_t", [NQ, 128, 512], F32, kind="ExternalInput").ap()
    sinq_d = nc.dram_tensor("sinq_t", [NQ, 128, 512], F32, kind="ExternalInput").ap()
    wq_d = nc.dram_tensor("wq_t", [128, DC * D], F32, kind="ExternalInput").ap()
    wk_d = nc.dram_tensor("wk_t", [128, DC * D], F32, kind="ExternalInput").ap()
    wv_d = nc.dram_tensor("wv_t", [128, DC * D], F32, kind="ExternalInput").ap()
    wg_d = nc.dram_tensor("wg_t", [128, DC * E], F32, kind="ExternalInput").ap()
    win_d = nc.dram_tensor("w_in_r", [HID, D], F32, kind="ExternalInput").ap()
    wot_d = nc.dram_tensor("w_out_tr", [HID, D], F32, kind="ExternalInput").ap()
    masks_d = nc.dram_tensor("masks", [NQ, 128, 512], BF16, kind="ExternalInput").ap()
    idb_d = nc.dram_tensor("ident_bf", [128, 128], BF16, kind="ExternalInput").ap()
    idf_d = nc.dram_tensor("ident_f32", [128, 128], F32, kind="ExternalInput").ap()
    out_d = nc.dram_tensor("out", [NQ, 128, D], F32, kind="ExternalOutput").ap()
    scratch_kind = "ExternalOutput" if DBG else "Internal"
    r1s_d = nc.dram_tensor("r1s", [NQ, 128, D], F32, kind=scratch_kind).ap()
    wcmy_t = nc.dram_tensor("wcmy", [DC * 128, D], BF16, kind="Internal")
    wcg_t = nc.dram_tensor("wcg", [E * DC * 128, D], BF16, kind="Internal")
    wcmy_d = wcmy_t.ap()
    wcg_d = wcg_t.ap()
    qts_d = nc.dram_tensor("qts", [NQ, 128, D], BF16, kind="Internal").ap()

    es0 = ExitStack()
    with es0:
        p = Prog(nc, es0)
        for sname in ("lda0", "lda1", "csl0", "csl1", "qld0", "qld1", "qst",
                      "ldk", "ldw", "ldwc", "ldc0", "ldc1",
                      "gldb0", "gldb1", "gst", "ccs", "str0", "str1", "sto0", "sto1",
                      "t", "v", "s", "gp",
                      *[f"lwc{e}" for e in range(E)]):
            p.sem(sname)

        def sb(name, shape, dt=F32):
            return es0.enter_context(nc.sbuf_tensor(name, shape, dt))

        ident_bf = sb("ident_bf_sb", [128, 128], BF16)
        ident_f = sb("ident_f_sb", [128, 128], F32)
        wg_sb = sb("wg_sb", [128, DC * E], F32)
        cstab_c = sb("cstab_c", [128, 1], F32)
        eps_c = sb("eps_c", [128, 1], F32)
        def mk_cs(tag):
            return {k: sb(f"cs_{tag}_{k}", [128, 1], F32)
                    for k in ("s", "q", "m", "bt", "veps", "rstd", "nmr")}
        cs_x = [mk_cs("x0"), mk_cs("x1")]
        cs_e = mk_cs("e")
        cs_e3 = mk_cs("e3")
        rstd_c = cs_e["rstd"]
        nmr_c = cs_e["nmr"]
        x_t = [sb(f"x_t{i}", [128, D]) for i in range(2)]
        cos_t = [sb(f"cos_t{i}", [128, 512]) for i in range(2)]
        sin_t = [sb(f"sin_t{i}", [128, 512]) for i in range(2)]
        sq_junk = sb("sq_junk", [128, D], BF16)

        block = es0.enter_context(nc.Block())

        em = Emitter()
        em.add("sy", lambda e: e.dma_start(out=ident_bf[:], in_=idb_d[:]).then_inc(p.sems["ldk"], 16))
        em.add("sy", lambda e: e.dma_start(out=ident_f[:], in_=idf_d[:]).then_inc(p.sems["ldk"], 16))
        em.add("sy", lambda e: e.dma_start(out=wg_sb[:], in_=wg_d[:]).then_inc(p.sems["ldk"], 16))
        CONST_LD = p.inc("ldk", 48)
        em.add("v", lambda e: e.memset(cstab_c[:], -CSTAB).then_inc(p.sems["v"], 1))
        p.inc("v")
        em.add("v", lambda e: e.memset(eps_c[:], EPS).then_inc(p.sems["v"], 1))
        p.inc("v")
        em.replay(block)

        def emit_ln_stats(em, x_ap, ready_sem, ready_tgt, cs, sq_dst):
            em.wait("v", p, ready_sem, ready_tgt)
            def _red(e, cs=cs):
                e.reduce_sum(cs["s"][:], x_ap, axis=AX.X)
                e.drain()
                e.tensor_scalar(cs["bt"][:], cs["s"][:], cs["s"][:], -1.0 / (D * D), ALU.mult, ALU.mult).then_inc(p.sems["v"], 1)
            em.add("v", _red)
            v_bt = p.inc("v")
            em.wait("s", p, ready_sem, ready_tgt)
            em.add("s", lambda e, cs=cs, sq_dst=sq_dst: e.activation(sq_dst, x_ap, AF.Square, accum_out=cs["q"][:]).then_inc(p.sems["s"], 1))
            s_sq = p.inc("s")
            em.wait("s", p, "v", v_bt)
            def _rstd(e, cs=cs):
                e.activation(cs["veps"][:], cs["q"][:], AF.Identity, bias=cs["bt"][:], scale=1.0 / D)
                e.drain()
                e.activation(cs["rstd"][:], cs["veps"][:], AF.Ln, bias=eps_c[:])
                e.drain()
                e.activation(cs["rstd"][:], cs["rstd"][:], AF.Exp, scale=-0.5).then_inc(p.sems["s"], 1)
            em.add("s", _rstd)
            s_rstd = p.inc("s")
            em.wait("g", p, "s", s_rstd)
            def _nmr(e, cs=cs):
                e.tensor_scalar(cs["nmr"][:], cs["s"][:], cs["rstd"][:], -1.0 / D, ALU.mult, ALU.mult).then_inc(p.sems["gp"], 1)
            em.add("g", _nmr)
            g_nmr = p.inc("gp")
            return ("gp", g_nmr), s_sq

        if "build" in PHASES:
            with ExitStack() as esB:
                def sbB(name, shape, dt=F32):
                    return esB.enter_context(nc.sbuf_tensor(name, shape, dt))
                win_hb = [sbB(f"win_hb{i}", [128, 8 * D], BF16) for i in range(2)]
                wot_hb = [sbB(f"wot_hb{i}", [128, 8 * D], BF16) for i in range(2)]
                w_acc = sbB("w_acc", [128, DC * D], F32)
                bld_ps = [esB.enter_context(nc.psum_tensor(f"bld_ps{i}", [128, 512 * 4], F32)) for i in range(2)]

                em = Emitter()
                pre_ld = {}
                pre_stats = {}
                for pt in range(2):
                    def _ldx0(eng, pt=pt):
                        eng.dma_start(out=x_t[pt][:], in_=xb_d[pt * 128:(pt + 1) * 128, :]).then_inc(p.sems[f"lda{pt}"], 16)
                        eng.dma_start(out=cos_t[pt][:], in_=cos_d[pt * 128:(pt + 1) * 128, :]).then_inc(p.sems[f"csl{pt}"], 16)
                        eng.dma_start(out=sin_t[pt][:], in_=sin_d[pt * 128:(pt + 1) * 128, :]).then_inc(p.sems[f"csl{pt}"], 16)
                    em.add("sy", _ldx0)
                    pre_ld[pt] = (p.inc(f"lda{pt}", 16), p.inc(f"csl{pt}", 32))
                grp_free_v = [0, 0]
                buf_free_t = [0, 0]
                for hb in range(4):
                    cur = hb % 2
                    gsem = f"gldb{cur}"
                    em.wait("g", p, "t", buf_free_t[cur])
                    def _ldb(eng, cur=cur, hb=hb, gsem=gsem):
                        eng.dma_start(
                            out=win_hb[cur][:].rearrange("p (hc d) -> p hc d", hc=8),
                            in_=win_d[hb * 1024:(hb + 1) * 1024, :].rearrange("(hc p) d -> p hc d", p=128),
                        ).then_inc(p.sems[gsem], 16)
                        eng.dma_start(
                            out=wot_hb[cur][:].rearrange("p (hc d) -> p hc d", hc=8),
                            in_=wot_d[hb * 1024:(hb + 1) * 1024, :].rearrange("(hc p) d -> p hc d", p=128),
                        ).then_inc(p.sems[gsem], 16)
                    em.add("g", _ldb)
                    g_ld = p.inc(gsem, 32)
                    em.wait("t", p, gsem, g_ld)
                    if hb == 2:
                        pre_stats[0] = emit_ln_stats(em, x_t[0][:], "lda0", pre_ld[0][0], cs_x[0], sq_junk[:])[0]
                        pre_stats[1] = emit_ln_stats(em, x_t[1][:], "lda1", pre_ld[1][0], cs_x[1], sq_junk[:])[0]
                    for oh in range(2):
                        for grp in range(2):
                            em.wait("t", p, "v", grp_free_v[grp])
                            def _mm(eng, cur=cur, oh=oh, grp=grp):
                                last = None
                                for dtl in range(4):
                                    dt_ = grp * 4 + dtl
                                    for hc in range(8):
                                        last = eng.matmul(
                                            bld_ps[grp][:, dtl * 512:(dtl + 1) * 512],
                                            win_hb[cur][:, hc * 1024 + dt_ * 128: hc * 1024 + dt_ * 128 + 128],
                                            wot_hb[cur][:, hc * 1024 + oh * 512: hc * 1024 + oh * 512 + 512],
                                            start=(hc == 0), stop=(hc == 7),
                                        )
                                last.then_inc(p.sems["t"], 1)
                            em.add("t", _mm)
                            t_mm = p.inc("t")
                            em.wait("v", p, "t", t_mm)
                            def _acc(eng, oh=oh, grp=grp, first=(hb == 0)):
                                eng.drain()
                                last = None
                                for dtl in range(4):
                                    dt_ = grp * 4 + dtl
                                    dst = w_acc[:, dt_ * 1024 + oh * 512: dt_ * 1024 + oh * 512 + 512]
                                    src = bld_ps[grp][:, dtl * 512:(dtl + 1) * 512]
                                    if first:
                                        last = eng.tensor_copy(dst, src)
                                    else:
                                        last = eng.tensor_tensor(dst, dst, src, ALU.add)
                                last.then_inc(p.sems["v"], 1)
                            em.add("v", _acc)
                            grp_free_v[grp] = p.inc("v")
                    buf_free_t[cur] = p.val("t")
                em.wait("g", p, "v", p.val("v"))
                def _st(eng):
                    for dc in range(DC):
                        eng.dma_start(out=wcmy_d[dc * 128:(dc + 1) * 128, :], in_=w_acc[:, dc * 1024:(dc + 1) * 1024]).then_inc(p.sems["gst"], 16)
                em.add("g", _st)
                p.inc("gst", 16 * DC)
                em.wait("g", p, "gst", p.val("gst"))
                def _cc(eng):
                    eng.collective_compute(
                        "AllGather",
                        mybir.AluOpType.bypass,
                        replica_groups=[list(range(8))],
                        ins=[wcmy_t.ap().opt()],
                        outs=[wcg_t.ap().opt()],
                    ).then_inc(p.sems["ccs"])
                em.add("g", _cc)
                p.inc("ccs", 1)
                em.replay(block)
        BUILD_T = p.val("t")
        BUILD_V = p.val("v")
        BUILD_GST = p.val("gst")

        r1_store_st = [0] * NQ
        if "a" in PHASES or "b" in PHASES:
            with ExitStack() as esA:
                def sbA(name, shape, dt=F32):
                    return esA.enter_context(nc.sbuf_tensor(name, shape, dt))
                kT_sb = sbA("kT_sb", [128, DC * S], BF16)
                v_sb = sbA("v_sb", [128, NT * D], BF16)
                wkv_sb = sbA("wkv_sb", [128, 2 * DC * D], BF16)
                xn_bf = [sbA(f"xn_bf{i}", [128, D], BF16) for i in range(2)]
                xnT_bf = sbA("xnT_bf", [128, D], BF16)
                rtmp = [sbA(f"rtmp{i}", [128, 512]) for i in range(4)]
                qT_bf = sbA("qT_bf", [128, D], BF16)
                masks_t = [sbA(f"masks_t{i}", [128, 512], BF16) for i in range(2)]
                a_row = [sbA(f"a_row{i}", [128, 512], BF16) for i in range(2)]
                a_tile4 = [sbA(f"a_tile4_{i}", [128, 512], BF16) for i in range(2)]
                rsum = sbA("rsum", [128, 8], F32)
                rinv_c = sbA("rinv_c", [128, 1], F32)
                av_sb = sbA("av_sb", [128, D], F32)
                trp = [esA.enter_context(nc.psum_tensor(f"trp{i}", [128, 1024], BF16)) for i in range(2)]
                kq_ps = esA.enter_context(nc.psum_tensor("kq_ps", [128, 1024], F32))
                va_ps = esA.enter_context(nc.psum_tensor("va_ps", [128, 1024], F32))
                s_ps = [esA.enter_context(nc.psum_tensor(f"s_ps{i}", [128, 512], F32)) for i in range(2)]

                em = Emitter()
                em.wait("s", p, "t", BUILD_T)
                em.wait("s", p, "v", BUILD_V)
                em.wait("s", p, "gst", BUILD_GST)
                em.wait("g", p, "v", BUILD_V)
                em.wait("g", p, "gst", BUILD_GST)
                def _ldw(eng):
                    eng.dma_start(out=wkv_sb[:, 0:DC * D], in_=wk_d[:]).then_inc(p.sems["ldw"], 16)
                    eng.dma_start(out=wkv_sb[:, DC * D:], in_=wv_d[:]).then_inc(p.sems["ldw"], 16)
                em.add("g", _ldw)
                g_w = p.inc("ldw", 32)

                trp_free = [("v", 0), ("v", 0)]
                kq_free_v = [0]
                va_free_s = [0]
                av_gp_prev = [0]
                rope_war = [0]
                xn_free_t = [0, 0]
                x_free = [(None, 0), (None, 0)]
                cs_free = [(None, 0), (None, 0)]
                mask_free = [(None, 0), (None, 0)]

                def emit_trans_evac(src_bf, dst_writer, gate, eng_ev="v"):
                    for gidx in range(2):
                        if gate:
                            em.wait("t", p, gate[0], gate[1])
                        fs, fv = trp_free[gidx]
                        if fv:
                            em.wait("t", p, fs, fv)
                        def _tr(eng, gidx=gidx, src_bf=src_bf):
                            last = None
                            for k in range(4):
                                dc = gidx * 4 + k
                                last = eng.transpose(trp[gidx][:, k * 128:(k + 1) * 128], src_bf[:, dc * 128:(dc + 1) * 128], ident_bf[:])
                            last.then_inc(p.sems["t"], 1)
                        em.add("t", _tr)
                        t_tr = p.inc("t")
                        em.wait(eng_ev, p, "t", t_tr)
                        def _ev(eng, gidx=gidx, dst_writer=dst_writer, eng_ev=eng_ev):
                            dst_writer(eng, gidx).then_inc(p.sems[eng_ev], 1)
                        em.add(eng_ev, _ev)
                        trp_free[gidx] = (eng_ev, p.inc(eng_ev))
                    return (eng_ev, p.val(eng_ev))

                def emit_trans_wait_s(s_gate):
                    em.wait("t", p, "s", s_gate)

                def emit_proj(w_off, ps, free_gate):
                    sem, tgt = free_gate
                    if tgt:
                        em.wait("t", p, sem, tgt)
                    def _proj(eng, w_off=w_off, ps=ps):
                        last = None
                        for oh in range(2):
                            for dc in range(DC):
                                last = eng.matmul(
                                    ps[:, oh * 512:(oh + 1) * 512],
                                    xnT_bf[:, dc * 128:(dc + 1) * 128],
                                    wkv_sb[:, w_off + dc * 1024 + oh * 512: w_off + dc * 1024 + oh * 512 + 512],
                                    start=(dc == 0), stop=(dc == 7),
                                )
                        last.then_inc(p.sems["t"], 1)
                    em.add("t", _proj)
                    return p.inc("t")

                def emit_rope(cur, t_proj, cs_gate=None):
                    em.wait("v", p, "t", t_proj)
                    if cs_gate:
                        em.wait("v", p, cs_gate[0], cs_gate[1])
                    if rope_war[0]:
                        em.wait("v", p, "gp", rope_war[0])
                    def _ropev(eng, cur=cur):
                        cos_ap, sin_ap = cos_t[cur][:], sin_t[cur][:]
                        eng.tensor_tensor(rtmp[0][:], kq_ps[:, 0:512], cos_ap, ALU.mult)
                        eng.tensor_tensor(rtmp[1][:], kq_ps[:, 512:1024], sin_ap, ALU.mult)
                        eng.tensor_tensor(rtmp[2][:], kq_ps[:, 0:512], sin_ap, ALU.mult)
                        eng.tensor_tensor(rtmp[3][:], kq_ps[:, 512:1024], cos_ap, ALU.mult).then_inc(p.sems["v"], 1)
                    em.add("v", _ropev)
                    v_mults = p.inc("v")
                    kq_free_v[0] = v_mults
                    cs_free[cur] = ("v", v_mults)
                    em.wait("g", p, "v", v_mults)
                    def _ropeg(eng, cur=cur):
                        xnb = xn_bf[cur]
                        eng.tensor_tensor(xnb[:, 0:512], rtmp[0][:], rtmp[1][:], ALU.subtract)
                        eng.tensor_tensor(xnb[:, 512:1024], rtmp[2][:], rtmp[3][:], ALU.add).then_inc(p.sems["gp"], 1)
                    em.add("g", _ropeg)
                    g_rope = p.inc("gp")
                    rope_war[0] = g_rope
                    return ("gp", g_rope)

                def emit_apply(cur, cs, gate, x_ap=None):
                    em.wait("s", p, gate[0], gate[1])
                    if xn_free_t[cur]:
                        em.wait("s", p, "t", xn_free_t[cur])
                    def _apply(e, cur=cur, cs=cs, x_ap=x_ap):
                        src_ap = x_ap if x_ap is not None else x_t[cur][:]
                        e.activation(xn_bf[cur][:], src_ap, AF.Identity, bias=cs["nmr"][:], scale=cs["rstd"][:]).then_inc(p.sems["s"], 1)
                    em.add("s", _apply)
                    return p.inc("s")

                if "a" in PHASES:
                    stats_v = {}
                    ld_tgts = {}
                    tv_of = {}
                    rope_of = {}

                    def emit_dma_a(pt):
                        cur = pt % 2
                        xsem, csem = f"lda{cur}", f"csl{cur}"
                        gate = x_free[cur]
                        if gate[0] is not None:
                            em.wait("sy", p, gate[0], gate[1])
                        def _ldx(eng, pt=pt, cur=cur, xsem=xsem):
                            eng.dma_start(out=x_t[cur][:], in_=xb_d[pt * 128:(pt + 1) * 128, :]).then_inc(p.sems[xsem], 16)
                        em.add("sy", _ldx)
                        x_tgt = p.inc(xsem, 16)
                        gate = cs_free[cur]
                        if gate[0] is not None:
                            em.wait("g", p, gate[0], gate[1])
                        def _ldcs(eng, pt=pt, cur=cur, csem=csem):
                            eng.dma_start(out=cos_t[cur][:], in_=cos_d[pt * 128:(pt + 1) * 128, :]).then_inc(p.sems[csem], 16)
                            eng.dma_start(out=sin_t[cur][:], in_=sin_d[pt * 128:(pt + 1) * 128, :]).then_inc(p.sems[csem], 16)
                        em.add("g", _ldcs)
                        cs_tgt = p.inc(csem, 32)
                        return x_tgt, cs_tgt

                    def _wr_kT_of(ppt):
                        def _wr(eng, gidx, ppt=ppt):
                            last = None
                            for k in range(4):
                                dc = gidx * 4 + k
                                last = eng.copy(kT_sb[:, dc * S + ppt * 128: dc * S + ppt * 128 + 128], trp[gidx][:, k * 128:(k + 1) * 128])
                            return last
                        return _wr

                    def _wr_xnT(eng, gidx):
                        last = None
                        for k in range(4):
                            dc = gidx * 4 + k
                            last = eng.tensor_copy(xnT_bf[:, dc * 128:(dc + 1) * 128], trp[gidx][:, k * 128:(k + 1) * 128])
                        return last

                    ld_tgts[0] = pre_ld[0]
                    ld_tgts[1] = pre_ld[1]
                    stats_v[0] = pre_stats[0]
                    stats_v[1] = pre_stats[1]
                    em.wait("t", p, "ldw", g_w)
                    em.wait("t", p, "ldk", CONST_LD)
                    for pt in range(NT):
                        cur = pt % 2
                        cs = cs_x[cur]
                        s_xn = emit_apply(cur, cs, stats_v[pt])
                        x_free[cur] = ("s", s_xn)
                        if 2 <= pt + 1 < NT:
                            ncur = (pt + 1) % 2
                            stats_v[pt + 1] = emit_ln_stats(em, x_t[ncur][:], f"lda{ncur}", ld_tgts[pt + 1][0], cs_x[ncur], sq_junk[:])[0]
                        if pt >= 1:
                            em.wait("s", p, "t", tv_of[pt - 1])
                            def _vev(eng, ppt=pt - 1):
                                eng.copy(v_sb[:, ppt * D:(ppt + 1) * D], va_ps[:]).then_inc(p.sems["s"], 1)
                            em.add("s", _vev)
                            va_free_s[0] = p.inc("s")
                        emit_trans_wait_s(s_xn)
                        g_xnT = emit_trans_evac(xn_bf[cur], _wr_xnT, None)
                        if pt >= 1:
                            emit_trans_evac(xn_bf[1 - cur], _wr_kT_of(pt - 1), rope_of[pt - 1], eng_ev="s")
                            xn_free_t[1 - cur] = p.val("t")
                        em.wait("t", p, g_xnT[0], g_xnT[1])
                        t_kproj = emit_proj(0, kq_ps, ("v", kq_free_v[0]))
                        tv_of[pt] = emit_proj(DC * D, va_ps, ("s", va_free_s[0]))
                        rope_of[pt] = emit_rope(cur, t_kproj, (f"csl{cur}", ld_tgts[pt][1]))
                        if pt + 2 < NT:
                            ld_tgts[pt + 2] = emit_dma_a(pt + 2)
                    emit_trans_evac(xn_bf[(NT - 1) % 2], _wr_kT_of(NT - 1), rope_of[NT - 1], eng_ev="s")
                    xn_free_t[(NT - 1) % 2] = p.val("t")
                    em.wait("s", p, "t", tv_of[NT - 1])
                    def _vev_last(eng):
                        eng.copy(v_sb[:, (NT - 1) * D:NT * D], va_ps[:]).then_inc(p.sems["s"], 1)
                    em.add("s", _vev_last)
                    va_free_s[0] = p.inc("s")
                PH_A_T = p.val("t")

                if "b" in PHASES:
                    if "a" not in PHASES:
                        em.wait("t", p, "ldw", g_w)
                        em.wait("t", p, "ldk", CONST_LD)
                    em.wait("g", p, "t", PH_A_T)
                    def _ldq(eng):
                        eng.dma_start(out=wkv_sb[:, 0:DC * D], in_=wq_d[:]).then_inc(p.sems["ldw"], 16)
                    em.add("g", _ldq)
                    g_wq = p.inc("ldw", 16)
                    a_row_free_t = [0, 0]
                    a_tile_free_t = [0, 0]
                    s_ps_free_s = [0, 0, 0, 0]
                    sps_def = [(s_ps[0], 0), (s_ps[1], 0), (kq_ps, 0), (kq_ps, 512)]
                    statsb_v = {}
                    ldb_tgts = {}

                    def emit_dma_b(j):
                        cur = j % 2
                        xsem, csem = f"lda{cur}", f"csl{cur}"
                        gate = x_free[cur]
                        if gate[0] is not None:
                            em.wait("sy", p, gate[0], gate[1])
                        def _ldx(eng, j=j, cur=cur, xsem=xsem):
                            eng.dma_start(out=x_t[cur][:], in_=xq_d[j]).then_inc(p.sems[xsem], 16)
                        em.add("sy", _ldx)
                        x_tgt = p.inc(xsem, 16)
                        for gate in (cs_free[cur], mask_free[cur]):
                            if gate[0] is not None:
                                em.wait("g", p, gate[0], gate[1])
                        def _ldcs(eng, j=j, cur=cur, csem=csem):
                            eng.dma_start(out=cos_t[cur][:], in_=cosq_d[j]).then_inc(p.sems[csem], 16)
                            eng.dma_start(out=sin_t[cur][:], in_=sinq_d[j]).then_inc(p.sems[csem], 16)
                            eng.dma_start(out=masks_t[cur][:], in_=masks_d[j]).then_inc(p.sems[csem], 16)
                        em.add("g", _ldcs)
                        cs_tgt = p.inc(csem, 48)
                        return x_tgt, cs_tgt

                    ldb_tgts[0] = emit_dma_b(0)
                    ldb_tgts[1] = emit_dma_b(1)
                    statsb_v[0] = emit_ln_stats(em, x_t[0][:], "lda0", ldb_tgts[0][0], cs_x[0], sq_junk[:])[0]
                    s_xn_of = {}
                    for j in range(NQ):
                        nblk = j // 2 + 1
                        nkt = 4 * nblk
                        cur = j % 2
                        cs = cs_x[cur]
                        if j == 0:
                            em.wait("t", p, "ldw", g_wq)
                        s_xn = s_xn_of.pop(j, None)
                        if s_xn is None:
                            s_xn = emit_apply(cur, cs, statsb_v[j])
                        if j + 1 < NQ:
                            ncur = (j + 1) % 2
                            statsb_v[j + 1] = emit_ln_stats(em, x_t[ncur][:], f"lda{ncur}", ldb_tgts[j + 1][0], cs_x[ncur], sq_junk[:])[0]
                            s_xn_of[j + 1] = emit_apply(ncur, cs_x[ncur], statsb_v[j + 1])
                        emit_trans_wait_s(s_xn)
                        def _wr_xnT(eng, gidx):
                            last = None
                            for k in range(4):
                                dc = gidx * 4 + k
                                last = eng.tensor_copy(xnT_bf[:, dc * 128:(dc + 1) * 128], trp[gidx][:, k * 128:(k + 1) * 128])
                            return last
                        g_xnT = emit_trans_evac(xn_bf[cur], _wr_xnT, None)
                        em.wait("t", p, g_xnT[0], g_xnT[1])
                        em.wait("t", p, "s", s_ps_free_s[2])
                        em.wait("t", p, "s", s_ps_free_s[3])
                        t_qproj = emit_proj(0, kq_ps, ("v", kq_free_v[0]))
                        g_rope = emit_rope(cur, t_qproj, (f"csl{cur}", ldb_tgts[j][1]))
                        def _wr_qT(eng, gidx):
                            last = None
                            for k in range(4):
                                dc = gidx * 4 + k
                                last = eng.tensor_copy(qT_bf[:, dc * 128:(dc + 1) * 128], trp[gidx][:, k * 128:(k + 1) * 128])
                            return last
                        g_qT = emit_trans_evac(xn_bf[cur], _wr_qT, g_rope)
                        v_qT = g_qT[1]
                        xn_free_t[cur] = p.val("t")
                        s_exp_last = 0
                        exp_of = {}

                        eva_of = {}

                        def emit_tra(blk):
                            tc = blk % 2
                            em.wait("t", p, "s", exp_of[blk])
                            fs, fv = trp_free[tc]
                            if fv:
                                em.wait("t", p, fs, fv)
                            def _tra(eng, blk=blk, tc=tc):
                                last = None
                                for sub in range(4):
                                    last = eng.transpose(trp[tc][:, sub * 128:(sub + 1) * 128], a_row[blk % 2][:, sub * 128:(sub + 1) * 128], ident_bf[:])
                                last.then_inc(p.sems["t"], 1)
                            em.add("t", _tra)
                            t_tra = p.inc("t")
                            a_row_free_t[blk % 2] = t_tra
                            em.wait("v", p, "t", t_tra)
                            em.wait("v", p, "t", a_tile_free_t[tc])
                            def _eva(eng, tc=tc):
                                eng.tensor_copy(a_tile4[tc][:], trp[tc][:, 0:512]).then_inc(p.sems["v"], 1)
                            em.add("v", _eva)
                            eva_of[blk] = p.inc("v")
                            trp_free[tc] = ("v", eva_of[blk])

                        def emit_av(blk, nkt=nkt):
                            tc = blk % 2
                            em.wait("t", p, "v", eva_of[blk])
                            if blk == 0:
                                em.wait("t", p, "s", va_free_s[0])
                            def _av(eng, blk=blk, tc=tc, nkt=nkt):
                                last = None
                                for sub in range(4):
                                    kt_g = blk * 4 + sub
                                    first = (kt_g == 0)
                                    lastf = (kt_g == nkt - 1)
                                    eng.matmul(va_ps[:, 0:512], a_tile4[tc][:, sub * 128:(sub + 1) * 128], v_sb[:, kt_g * D: kt_g * D + 512], start=first, stop=lastf)
                                    last = eng.matmul(va_ps[:, 512:1024], a_tile4[tc][:, sub * 128:(sub + 1) * 128], v_sb[:, kt_g * D + 512: kt_g * D + 1024], start=first, stop=lastf)
                                last.then_inc(p.sems["t"], 1)
                            em.add("t", _av)
                            a_tile_free_t[tc] = p.inc("t")

                        for blk in range(nblk):
                            bc = blk % 4
                            ar = blk % 2
                            spt, spo = sps_def[bc]
                            em.wait("t", p, "v", v_qT)
                            em.wait("t", p, "s", s_ps_free_s[bc])
                            last_blk = (blk == nblk - 1)
                            def _sc(eng, blk=blk, cur=cur, last_blk=last_blk, spt=spt, spo=spo):
                                last = None
                                for dc in range(DC):
                                    last = eng.matmul(
                                        spt[:, spo:spo + 512],
                                        qT_bf[:, dc * 128:(dc + 1) * 128],
                                        kT_sb[:, dc * S + blk * 512: dc * S + blk * 512 + 512],
                                        start=(dc == 0), stop=(dc == 7 and not last_blk),
                                    )
                                if last_blk:
                                    for sub in range(4):
                                        last = eng.matmul(
                                            spt[:, spo + sub * 128:spo + (sub + 1) * 128],
                                            masks_t[cur][:, sub * 128:(sub + 1) * 128],
                                            ident_bf[:],
                                            start=False, stop=(sub == 3),
                                        )
                                last.then_inc(p.sems["t"], 1)
                            em.add("t", _sc)
                            t_sc = p.inc("t")
                            if last_blk:
                                mask_free[cur] = ("t", t_sc)
                            em.wait("s", p, "t", t_sc)
                            em.wait("s", p, "t", a_row_free_t[ar])
                            def _exp(eng, blk=blk, ar=ar, spt=spt, spo=spo):
                                eng.activation(a_row[ar][:], spt[:, spo:spo + 512], AF.Exp,
                                               bias=cstab_c[:], scale=INV_SQRT_D,
                                               accum_out=rsum[:, blk:blk + 1]).then_inc(p.sems["s"], 1)
                            em.add("s", _exp)
                            s_exp = p.inc("s")
                            exp_of[blk] = s_exp
                            s_ps_free_s[bc] = s_exp
                            s_exp_last = s_exp
                            if blk >= 1:
                                emit_tra(blk - 1)
                            if blk >= 2:
                                emit_av(blk - 2)
                        emit_tra(nblk - 1)
                        if nblk >= 2:
                            emit_av(nblk - 2)
                        emit_av(nblk - 1)
                        t_av_done = p.val("t")
                        em.wait("v", p, "s", s_exp_last)
                        def _rs(eng, nblk=nblk):
                            eng.reduce_sum(rinv_c[:], rsum[:, 0:nblk], axis=AX.X)
                            eng.drain()
                            eng.reciprocal(rinv_c[:], rinv_c[:]).then_inc(p.sems["v"], 1)
                        em.add("v", _rs)
                        v_rinv = p.inc("v")
                        em.wait("s", p, "v", v_rinv)
                        em.wait("s", p, "t", t_av_done)
                        if av_gp_prev[0]:
                            em.wait("s", p, "gp", av_gp_prev[0])
                        def _avev(eng):
                            eng.activation(av_sb[:], va_ps[:], AF.Identity, scale=rinv_c[:]).then_inc(p.sems["s"], 1)
                        em.add("s", _avev)
                        s_avev = p.inc("s")
                        va_free_s[0] = s_avev
                        if DBG and "slot0" in DBG_DUMP and j == 0:
                            d0 = nc.dram_tensor("dbg_s0", [128, 1024 + 512 + 1024 + 16], F32, kind="ExternalOutput").ap()
                            d0b = nc.dram_tensor("dbg_s0b", [128, 1536], BF16, kind="ExternalOutput").ap()
                            em.wait("sy", p, "s", s_avev)
                            em.wait("sy", p, "v", p.val("v"))
                            def _dmp(eng):
                                eng.dma_start(out=d0b[:, 0:1024], in_=qT_bf[:]).then_inc(p.sems["sto0"], 16)
                                eng.dma_start(out=d0b[:, 1024:1536], in_=a_row[0][:]).then_inc(p.sems["sto0"], 16)
                                eng.dma_start(out=d0[:, 0:1024], in_=av_sb[:]).then_inc(p.sems["sto0"], 16)
                                eng.dma_start(out=d0[:, 1024:1032], in_=rsum[:], allow_slow_non_contiguous=True).then_inc(p.sems["sto0"], 16)
                                eng.dma_start(out=d0[:, 1032:1033], in_=rinv_c[:], allow_slow_non_contiguous=True).then_inc(p.sems["sto0"], 16)
                            em.add("sy", _dmp)
                            dump_tgt = p.inc("sto0", 80)
                            for engx in ("v", "s", "t"):
                                em.wait(engx, p, "sto0", dump_tgt)
                        g_nmr_av, _ = emit_ln_stats(em, av_sb[:], "s", s_avev, cs_e, sq_junk[:])
                        av_gp_prev[0] = g_nmr_av[1]
                        em.wait("v", p, g_nmr_av[0], g_nmr_av[1])
                        em.wait("v", p, "gp", rope_war[0])
                        def _r1(eng, cur=cur):
                            eng.tensor_scalar(rtmp[0][:], x_t[cur][:, 0:512], cs_e["nmr"][:], None, ALU.add)
                            eng.tensor_scalar(rtmp[1][:], x_t[cur][:, 512:1024], cs_e["nmr"][:], None, ALU.add)
                            eng.drain()
                            eng.scalar_tensor_tensor(x_t[cur][:, 0:512], av_sb[:, 0:512], cs_e["rstd"][:], rtmp[0][:], ALU.mult, ALU.add)
                            eng.scalar_tensor_tensor(x_t[cur][:, 512:1024], av_sb[:, 512:1024], cs_e["rstd"][:], rtmp[1][:], ALU.mult, ALU.add).then_inc(p.sems["v"], 1)
                        em.add("v", _r1)
                        v_r1 = p.inc("v")
                        em.wait("s", p, "v", v_r1)
                        def _str1(eng, j=j, cur=cur):
                            eng.dma_start(out=r1s_d[j], in_=x_t[cur][:]).then_inc(p.sems["str0"], 16)
                        em.add("s", _str1)
                        r1_store_st[j] = p.inc("str0", 16)
                        x_free[cur] = ("str0", r1_store_st[j])
                        if j + 2 < NQ:
                            ldb_tgts[j + 2] = emit_dma_b(j + 2)
                if DBG and "kv" in DBG_DUMP:
                    kv_d = nc.dram_tensor("dbg_kv", [2, 128, DC * S], BF16, kind="ExternalOutput").ap()
                    em.wait("sy", p, "t", p.val("t"))
                    em.wait("sy", p, "v", p.val("v"))
                    em.wait("sy", p, "s", p.val("s"))
                    def _dump(eng):
                        eng.dma_start(out=kv_d[0], in_=kT_sb[:]).then_inc(p.sems["sto0"], 16)
                        eng.dma_start(out=kv_d[1], in_=v_sb[:]).then_inc(p.sems["sto0"], 16)
                    em.add("sy", _dump)
                    p.inc("sto0", 32)
                em.replay(block)
        PH_B_T, PH_B_V, PH_B_S = p.val("t"), p.val("v"), p.val("s")

        if "c" in PHASES:
            with ExitStack() as esC:
                def sbC(name, shape, dt=F32):
                    return esC.enter_context(nc.sbuf_tensor(name, shape, dt))
                wc_sb = sbC("wc_sb", [128, E * DC * D], BF16)
                xhT_bf = sbC("xhT_bf", [128, DC * 2 * 128], BF16)
                gates = sbC("gates", [128, NQ * E], F32)
                r1c_t = x_t
                xh2_t = sbC("xh2_t", [128, D], F32)
                x2T_f = sbC("x2T_f", [128, D], F32)
                lg_sb = sbC("lg_sb", [128, E], F32)
                g_m1 = sbC("g_m1", [128, 1], F32)
                g_nm1 = sbC("g_nm1", [128, 1], F32)
                g_eq = sbC("g_eq", [128, E], F32)
                g_lg2 = sbC("g_lg2", [128, E], F32)
                g_m2 = sbC("g_m2", [128, 1], F32)
                g_ee = sbC("g_ee", [128, E], F32)
                g_dex = sbC("g_dex", [128, 1], F32)
                g_sel = sbC("g_sel", [128, E], F32)
                p_acc = [sbC(f"p_acc{i}", [128, D]) for i in range(2)]
                out_t = [sbC(f"out_t{i}", [128, D]) for i in range(2)]
                xmb2_t = sbC("xmb2_t", [128, D], F32)
                trf = [esC.enter_context(nc.psum_tensor(f"trf{i}", [128, 512], F32)) for i in range(2)]
                lg_ps = esC.enter_context(nc.psum_tensor("lg_ps", [128, 512], F32))
                pp_ps = [esC.enter_context(nc.psum_tensor(f"pp_ps{i}", [128, 1024], F32)) for i in range(2)]

                em = Emitter()
                em.wait("g", p, "t", PH_B_T)
                em.wait("g", p, "v", PH_B_V)
                em.wait("g", p, "s", PH_B_S)
                em.wait("g", p, "ccs", 1)
                def _ldwc(eng):
                    for e_i in range(E):
                        eng.dma_start(
                            out=wc_sb[:, e_i * DC * D:(e_i + 1) * DC * D].rearrange("p (dc d) -> p dc d", dc=DC),
                            in_=wcg_d[e_i * DC * 128:(e_i + 1) * DC * 128, :].rearrange("(dc q) d -> q dc d", q=128),
                        ).then_inc(p.sems[f"lwc{e_i}"], 16)
                em.add("g", _ldwc)
                for e_i in range(E):
                    p.inc(f"lwc{e_i}", 16)

                trf_free_v = [0, 0]
                r1c_free_v = [0, 0]
                pp_free_v = [0, 0]
                lg_free_v = [0]
                out_free_st = [0, 0]
                sq_ln3_prev = [0]
                gp_ln3_prev = [0]
                xh2_free_t = [0]
                x2T_free_t = [0]
                x2T_free_s = [0]
                xh_free_t = [0, 0]

                def emit_c_prologue(j):
                    cur = j % 2
                    csem = f"ldc{cur}"
                    if j < 2:
                        em.wait("sy", p, "str0", r1_store_st[NQ - 2 + j])
                    else:
                        em.wait("sy", p, "str0", r1_store_st[j])
                    em.wait("sy", p, "v", r1c_free_v[cur])
                    def _ldr(eng, j=j, cur=cur):
                        eng.dma_start(out=r1c_t[cur][:], in_=r1s_d[j]).then_inc(p.sems[csem], 16)
                    em.add("sy", _ldr)
                    ld_r1 = p.inc(csem, 16)
                    gate2, _ = emit_ln_stats(em, r1c_t[cur][:], csem, ld_r1, cs_e, sq_junk[:])
                    em.wait("s", p, gate2[0], gate2[1])
                    if xh2_free_t[0]:
                        em.wait("s", p, "t", xh2_free_t[0])
                    def _ap2(eng, cur=cur):
                        eng.activation(xh2_t[:], r1c_t[cur][:], AF.Identity, bias=nmr_c[:], scale=rstd_c[:]).then_inc(p.sems["s"], 1)
                    em.add("s", _ap2)
                    s_xh2 = p.inc("s")
                    for gidx in range(2):
                        em.wait("t", p, "s", s_xh2)
                        em.wait("t", p, "v", trf_free_v[gidx])
                        def _trf(eng, gidx=gidx):
                            last = None
                            for k in range(4):
                                dc = gidx * 4 + k
                                last = eng.transpose(trf[gidx][:, k * 128:(k + 1) * 128], xh2_t[:, dc * 128:(dc + 1) * 128], ident_f[:])
                            last.then_inc(p.sems["t"], 1)
                        em.add("t", _trf)
                        t_trf = p.inc("t")
                        em.wait("v", p, "t", t_trf)
                        if x2T_free_t[0]:
                            em.wait("v", p, "t", x2T_free_t[0])
                        if x2T_free_s[0]:
                            em.wait("v", p, "s", x2T_free_s[0])
                        def _evf(eng, gidx=gidx):
                            last = None
                            for k in range(4):
                                dc = gidx * 4 + k
                                last = eng.tensor_copy(x2T_f[:, dc * 128:(dc + 1) * 128], trf[gidx][:, k * 128:(k + 1) * 128])
                            last.then_inc(p.sems["v"], 1)
                        em.add("v", _evf)
                        trf_free_v[gidx] = p.inc("v")
                    xh2_free_t[0] = p.val("t")
                    v_x2T = p.val("v")
                    em.wait("s", p, "v", v_x2T)
                    if xh_free_t[j % 2]:
                        em.wait("s", p, "t", xh_free_t[j % 2])
                    def _cast(eng, j=j):
                        last = None
                        for dc in range(DC):
                            last = eng.copy(xhT_bf[:, dc * 256 + (j % 2) * 128: dc * 256 + (j % 2) * 128 + 128], x2T_f[:, dc * 128:(dc + 1) * 128])
                        last.then_inc(p.sems["s"], 1)
                    em.add("s", _cast)
                    s_xh = p.inc("s")
                    em.wait("t", p, "v", v_x2T)
                    em.wait("t", p, "v", lg_free_v[0])
                    if j == 0:
                        em.wait("t", p, "ldk", CONST_LD)
                    def _lg(eng):
                        last = None
                        for dc in range(DC):
                            last = eng.matmul(lg_ps[:, 0:E], x2T_f[:, dc * 128:(dc + 1) * 128], wg_sb[:, dc * E:(dc + 1) * E], start=(dc == 0), stop=(dc == 7))
                        last.then_inc(p.sems["t"], 1)
                    em.add("t", _lg)
                    t_lg = p.inc("t")
                    x2T_free_t[0] = t_lg
                    x2T_free_s[0] = s_xh
                    em.wait("v", p, "t", t_lg)
                    def _gate(eng, j=j):
                        eng.tensor_copy(lg_sb[:], lg_ps[:, 0:E])
                        eng.drain()
                        eng.reduce_max(g_m1[:], lg_sb[:], axis=AX.X)
                        eng.drain()
                        eng.tensor_scalar_mul(g_nm1[:], g_m1[:], -1.0)
                        eng.tensor_scalar(g_eq[:], lg_sb[:], g_m1[:], None, ALU.is_equal)
                        eng.drain()
                        eng.scalar_tensor_tensor(g_lg2[:], g_eq[:], -1e30, lg_sb[:], ALU.mult, ALU.add)
                        eng.drain()
                        eng.reduce_max(g_m2[:], g_lg2[:], axis=AX.X).then_inc(p.sems["v"], 1)
                    em.add("v", _gate)
                    v_g1 = p.inc("v")
                    lg_free_v[0] = v_g1
                    em.wait("s", p, "v", v_g1)
                    def _gexp(eng):
                        eng.activation(g_ee[:], lg_sb[:], AF.Exp, bias=g_nm1[:])
                        eng.drain()
                        eng.activation(g_dex[:], g_m2[:], AF.Exp, bias=g_nm1[:]).then_inc(p.sems["s"], 1)
                    em.add("s", _gexp)
                    s_gexp = p.inc("s")
                    em.wait("v", p, "s", s_gexp)
                    def _gfin(eng, j=j):
                        eng.drain()
                        eng.tensor_scalar_add(g_dex[:], g_dex[:], 1.0)
                        eng.drain()
                        eng.reciprocal(g_dex[:], g_dex[:])
                        eng.tensor_scalar(g_sel[:], lg_sb[:], g_m2[:], None, ALU.is_ge)
                        eng.drain()
                        eng.tensor_tensor(g_ee[:], g_ee[:], g_sel[:], ALU.mult)
                        eng.drain()
                        eng.tensor_scalar(gates[:, j * E:(j + 1) * E], g_ee[:], g_dex[:], None, ALU.mult).then_inc(p.sems["v"], 1)
                    em.add("v", _gfin)
                    v_gates = p.inc("v")
                    return s_xh, v_gates

                sxh_j, vgates_j = emit_c_prologue(0)
                for j in range(NQ):
                    cur = j % 2
                    for e_i in range(E):
                        pc = e_i % 2
                        em.wait("t", p, "v", pp_free_v[pc])
                        em.wait("t", p, "s", sxh_j)
                        if j == 0:
                            em.wait("t", p, f"lwc{e_i}", 16)
                        def _pmm(eng, j=j, e_i=e_i, pc=pc):
                            last = None
                            for oh in range(2):
                                for dc in range(DC):
                                    last = eng.matmul(
                                        pp_ps[pc][:, oh * 512:(oh + 1) * 512],
                                        xhT_bf[:, dc * 256 + (j % 2) * 128: dc * 256 + (j % 2) * 128 + 128],
                                        wc_sb[:, (e_i * DC + dc) * D + oh * 512: (e_i * DC + dc) * D + oh * 512 + 512],
                                        start=(dc == 0), stop=(dc == 7),
                                    )
                            last.then_inc(p.sems["t"], 1)
                        em.add("t", _pmm)
                        t_pmm = p.inc("t")
                        em.wait("v", p, "t", t_pmm)
                        em.wait("v", p, "v", vgates_j)
                        if e_i == 1 and sq_ln3_prev[0] > 0:
                            em.wait("v", p, "s", sq_ln3_prev[0])
                            em.wait("v", p, "gp", gp_ln3_prev[0])
                        def _pacc(eng, j=j, e_i=e_i, pc=pc):
                            eng.drain()
                            gcol = gates[:, j * E + e_i: j * E + e_i + 1]
                            if e_i == 0:
                                eng.tensor_scalar(p_acc[0][:], pp_ps[0][:], gcol, None, ALU.mult).then_inc(p.sems["v"], 1)
                            else:
                                eng.scalar_tensor_tensor(p_acc[e_i % 2][:], pp_ps[pc][:], gcol, p_acc[(e_i + 1) % 2][:], ALU.mult, ALU.add).then_inc(p.sems["v"], 1)
                        em.add("v", _pacc)
                        pp_free_v[pc] = p.inc("v")
                    xh_free_t[j % 2] = p.val("t")
                    v_pdone = p.val("v")
                    if j + 1 < NQ:
                        sxh_j, vgates_j = emit_c_prologue(j + 1)
                    gate3, s_sq3 = emit_ln_stats(em, p_acc[1][:], "v", v_pdone, cs_e3, sq_junk[:])
                    sq_ln3_prev[0] = s_sq3
                    gp_ln3_prev[0] = gate3[1]
                    em.wait("v", p, gate3[0], gate3[1])
                    if out_free_st[cur] > 0:
                        em.wait("v", p, f"sto{cur}", out_free_st[cur])
                    def _fin(eng, cur=cur):
                        eng.drain()
                        eng.tensor_scalar(xmb2_t[:], r1c_t[cur][:], cs_e3["nmr"][:], None, ALU.add)
                        eng.drain()
                        eng.scalar_tensor_tensor(out_t[cur][:], p_acc[1][:], cs_e3["rstd"][:], xmb2_t[:], ALU.mult, ALU.add).then_inc(p.sems["v"], 1)
                    em.add("v", _fin)
                    v_fin = p.inc("v")
                    r1c_free_v[cur] = v_fin
                    em.wait("s", p, "v", v_fin)
                    def _sto(eng, j=j, cur=cur):
                        eng.dma_start(out=out_d[j], in_=out_t[cur][:]).then_inc(p.sems[f"sto{cur}"], 16)
                    em.add("s", _sto)
                    out_free_st[cur] = p.inc(f"sto{cur}", 16)
                for sname in ("sto0", "sto1", "str0", "str1", "gst"):
                    em.wait("s", p, sname, p.val(sname))
                em.replay(block)
        else:
            em = Emitter()
            for sname in ("sto0", "sto1", "str0", "str1", "gst"):
                em.wait("s", p, sname, p.val(sname))
            em.replay(block)

    return nc


def _host_prep(L0_attn_x, L0_attn_Wq, L0_attn_Wk, L0_attn_Wv, L0_ffn_w_gate, L0_ffn_w_in, L0_ffn_w_out):
    x = np.asarray(L0_attn_x, dtype=np.float32)
    Wq = np.asarray(L0_attn_Wq, dtype=np.float32)
    Wk = np.asarray(L0_attn_Wk, dtype=np.float32)
    Wv = np.asarray(L0_attn_Wv, dtype=np.float32)
    wg = np.asarray(L0_ffn_w_gate, dtype=np.float32)
    w_in = np.ascontiguousarray(np.asarray(L0_ffn_w_in, dtype=np.float32))
    w_out = np.asarray(L0_ffn_w_out, dtype=np.float32)

    perm = np.concatenate([np.arange(0, D, 2), np.arange(1, D, 2)])

    def chunk_w(wt):
        return np.ascontiguousarray(wt.reshape(DC, 128, wt.shape[1]).transpose(1, 0, 2).reshape(128, DC * wt.shape[1]))

    wq_t = chunk_w(Wq.T[:, perm])
    wk_t = chunk_w(Wk.T[:, perm])
    wv_t = chunk_w(np.ascontiguousarray(Wv.T))
    wg_t = chunk_w(np.ascontiguousarray(wg.T))

    pos = np.arange(S, dtype=np.float64)
    inv = 10000.0 ** (-np.arange(0, D, 2, dtype=np.float64) / D)
    ang = pos[:, None] * inv[None, :]
    cos_t = np.cos(ang).astype(np.float32)
    sin_t = np.sin(ang).astype(np.float32)

    ident_bf = np.eye(128, dtype=ml_dtypes.bfloat16)
    ident_f = np.eye(128, dtype=np.float32)

    kk = np.arange(128)
    causalT = np.where(kk[:, None] <= kk[None, :], 0.0, -1e30).astype(ml_dtypes.bfloat16)
    allT = np.full((128, 128), -1e30, dtype=np.float32).astype(ml_dtypes.bfloat16)

    in_maps = []
    qlists = []
    for c in range(8):
        b, par = c // 2, c % 2
        qt = qtiles_for(par)
        qlists.append(qt)
        xb = np.ascontiguousarray(x[b])
        xq = np.ascontiguousarray(xb.reshape(NT, 128, D)[qt])
        cosq = np.ascontiguousarray(cos_t.reshape(NT, 128, 512)[qt])
        sinq = np.ascontiguousarray(sin_t.reshape(NT, 128, 512)[qt])
        masks = np.zeros((NQ, 128, 512), dtype=ml_dtypes.bfloat16)
        for jj, g in enumerate(qt):
            nblk = jj // 2 + 1
            base = (nblk - 1) * 4
            for sub in range(4):
                kt = base + sub
                if kt == g:
                    masks[jj, :, sub * 128:(sub + 1) * 128] = causalT
                elif kt > g:
                    masks[jj, :, sub * 128:(sub + 1) * 128] = allT
        in_maps.append({
            "xb": xb, "xq": xq,
            "cos_t": cos_t, "sin_t": sin_t,
            "cosq_t": cosq, "sinq_t": sinq,
            "wq_t": wq_t, "wk_t": wk_t, "wv_t": wv_t, "wg_t": wg_t,
            "w_in_r": np.ascontiguousarray(w_in[c]),
            "w_out_tr": np.ascontiguousarray(w_out[c].T),
            "masks": masks,
            "ident_bf": ident_bf, "ident_f32": ident_f,
        })
    return in_maps, qlists


def kernel(L0_attn_x, L0_attn_Wq, L0_attn_Wk, L0_attn_Wv, L0_ffn_w_gate, L0_ffn_w_in, L0_ffn_w_out):
    in_maps, qlists = _host_prep(L0_attn_x, L0_attn_Wq, L0_attn_Wk, L0_attn_Wv,
                                 L0_ffn_w_gate, L0_ffn_w_in, L0_ffn_w_out)
    nc = build_nc()
    res = run_bass_kernel_spmd(nc, in_maps, core_ids=list(range(8)))
    out = np.empty((B, S, D), dtype=np.float32)
    for c in range(8):
        b = c // 2
        o = res.results[c]["out"]
        for jj, g in enumerate(qlists[c]):
            out[b, g * 128:(g + 1) * 128, :] = o[jj]
    return out
